# version 03
# baseline (speedup 1.0000x reference)
import numpy as np
import ml_dtypes
import concourse.bass as bass
import concourse.mybir as mybir
from concourse.bass_utils import run_bass_kernel_spmd

F32 = mybir.dt.float32
BF16 = mybir.dt.bfloat16
I32 = mybir.dt.int32
AF = mybir.ActivationFunctionType
ALU = mybir.AluOpType
AX = mybir.AxisListType
EPS = 1e-5
H, D, BLKT, DM, KC = 32, 128, 256, 4096, 32


def decay_consts():
    base = 1.0 / 2.0 ** (8.0 / H)
    expo = np.arange(H, dtype=np.float64) + 1.0
    factor = 1.0 - 1 / (32 - 1 + 1e-05) + 1e-05
    slope = base ** expo * factor
    r = np.arange(BLKT, dtype=np.float64) + 1.0
    q_decay = np.exp(-slope[:, None] * r[None, :])
    k_decay = np.exp(-slope[:, None] * (BLKT - r)[None, :])
    diff = r[:, None] - r[None, :]
    diag = slope[:, None, None] * diff[None]
    diag = np.where(diag >= 0, -diag, -np.inf)
    diag_decay = np.exp(diag)
    block_decay = np.exp(-slope * BLKT)
    return q_decay, k_decay, diag_decay, block_decay


class Ct:
    def __init__(self, nc, name):
        self.h = nc.alloc_semaphore(name)
        self.n = 0

    def inc(self, instr, k=1):
        instr.then_inc(self.h, k)
        self.n += k
        return instr

    def wait(self, eng, at=None):
        eng.wait_ge(self.h, self.n if at is None else at)


def build(cfg):
    NCO = cfg.get("nco", 16)
    DBG_RAW = bool(cfg.get("dbg_raw"))
    NCP = cfg.get("ncp", 16)
    dump = cfg.get("dump", [])
    TOKO, TOKP = NCO * 128, NCP * 128
    NBO = TOKO // BLKT
    NBP = TOKP // BLKT if NCP else 0

    _, _, _, bdec_np = decay_consts()
    bdecs = [float(b) for b in bdec_np]

    nc = bass.Bass("TRN2", target_bir_lowering=False, dynamic_dma_scratch_size=16384)

    x_own = nc.dram_tensor("x_own", [TOKO, DM], F32, kind="ExternalInput").ap()
    x_pref = nc.dram_tensor("x_pref", [max(TOKP, 128), DM], F32, kind="ExternalInput").ap()
    gamma = nc.dram_tensor("gamma", [128, 1], F32, kind="ExternalInput").ap()
    wqkvT = nc.dram_tensor("wqkvT", [DM, 12288], F32, kind="ExternalInput").ap()
    wgT = nc.dram_tensor("wgT", [DM, DM], F32, kind="ExternalInput").ap()
    woT = nc.dram_tensor("woT", [DM, DM], F32, kind="ExternalInput").ap()
    packT = nc.dram_tensor("packT", [512, DM], I32, kind="ExternalInput").ap()
    scaleT = nc.dram_tensor("scaleT", [32, DM], F32, kind="ExternalInput").ap()
    nrmw = nc.dram_tensor("nrmw", [128, KC], F32, kind="ExternalInput").ap()
    ddc = nc.dram_tensor("ddc", [H * 256, 256], BF16, kind="ExternalInput").ap()
    kdec = nc.dram_tensor("kdec", [128, 2 * H], F32, kind="ExternalInput").ap()
    qdec = nc.dram_tensor("qdec", [128, 2 * H], F32, kind="ExternalInput").ap()
    ident_in = nc.dram_tensor("identin", [128, 128], BF16, kind="ExternalInput").ap()
    fbias_bf = nc.dram_tensor("fbias_bf", [1, DM], BF16, kind="ExternalInput").ap()
    y_out = nc.dram_tensor("y", [TOKO, DM], F32, kind="ExternalOutput").ap()

    wdt_d = nc.dram_tensor("wdt_d", [DM, DM], BF16).ap()
    x1_d = nc.dram_tensor("x1_d", [TOKO, DM], F32).ap()
    x2_d = nc.dram_tensor("x2_d", [TOKO, DM], F32).ap()
    q_d = nc.dram_tensor("q_d", [TOKO, DM], BF16).ap()
    k_d = nc.dram_tensor("k_d", [TOKO, DM], BF16).ap()
    v_d = nc.dram_tensor("v_d", [TOKO, DM], BF16).ap()
    g_d = nc.dram_tensor("g_d", [TOKO, DM], BF16).ap()
    gated_d = nc.dram_tensor("gated_d", [TOKO, DM], BF16).ap()
    scr_map = {"x1_d": x1_d, "x2_d": x2_d, "q_d": q_d, "k_d": k_d, "v_d": v_d,
               "g_d": g_d, "gated_d": gated_d, "wdt_d": wdt_d}
    if NCP:
        x1p_d = nc.dram_tensor("x1p_d", [TOKP, DM], F32).ap()
        kp_d = nc.dram_tensor("kp_d", [TOKP, DM], BF16).ap()
        vp_d = nc.dram_tensor("vp_d", [TOKP, DM], BF16).ap()
        scr_map.update({"x1p_d": x1p_d, "kp_d": kp_d, "vp_d": vp_d})
    dump_outs = {}
    for name in dump:
        if name == "arena":
            continue
        src = scr_map[name]
        dump_outs[name] = nc.dram_tensor(
            "dump_" + name, list(src.shape), src.dtype, kind="ExternalOutput").ap()
    arena_out = (nc.dram_tensor("dump_arena", [128, 65536], BF16,
                                kind="ExternalOutput").ap() if "arena" in dump else None)

    arena = nc.alloc_sbuf_tensor("arena", [128, 65536], BF16)
    kvst = nc.alloc_sbuf_tensor("kvst", [128, 4096], F32)
    kvb = nc.alloc_sbuf_tensor("kvb", [128, 4096], BF16)
    wstream = nc.alloc_sbuf_tensor("wstream", [128, 20480], BF16)
    scratch = nc.alloc_sbuf_tensor("scratch", [128, 2560], F32)
    bn = nc.alloc_sbuf_tensor("bn", [128, 48], F32)
    bnag = nc.alloc_sbuf_tensor("bnag", [128, 2], F32)
    t1 = nc.alloc_sbuf_tensor("t1", [128, 1], F32)
    t2 = nc.alloc_sbuf_tensor("t2", [128, 1], F32)
    t3 = nc.alloc_sbuf_tensor("t3", [128, 1], F32)
    t4 = nc.alloc_sbuf_tensor("t4", [128, 1], F32)
    t5 = nc.alloc_sbuf_tensor("t5", [128, 1], F32)
    t6 = nc.alloc_sbuf_tensor("t6", [128, 1], F32)
    lnsum = nc.alloc_sbuf_tensor("lnsum", [128, 16], F32)
    lnsq = nc.alloc_sbuf_tensor("lnsq", [128, 16], F32)
    lnr = nc.alloc_sbuf_tensor("lnr", [128, 16], F32)
    lnb = nc.alloc_sbuf_tensor("lnb", [128, 16], F32)
    lntmp = nc.alloc_sbuf_tensor("lntmp", [128, 16], F32)
    rsum = nc.alloc_sbuf_tensor("rsum", [128, 16], F32)
    sqcol = nc.alloc_sbuf_tensor("sqcol", [128, 4], F32)
    kdec_s = nc.alloc_sbuf_tensor("kdec_s", [128, 2 * H], F32)
    qdec_s = nc.alloc_sbuf_tensor("qdec_s", [128, 2 * H], F32)
    gam_s = nc.alloc_sbuf_tensor("gam_s", [128, 1], F32)
    nrmw_s = nc.alloc_sbuf_tensor("nrmw_s", [128, KC], F32)
    ident = nc.alloc_sbuf_tensor("ident", [128, 128], BF16)
    ones1 = nc.alloc_sbuf_tensor("ones1", [1, 128], BF16)

    pA = nc.alloc_psum_tensor("pA", [128, 512], F32)
    pB = nc.alloc_psum_tensor("pB", [128, 512], F32)
    pS0 = nc.alloc_psum_tensor("pS0", [128, 512], F32)
    pC = nc.alloc_psum_tensor("pC", [128, 512], F32)
    pD = nc.alloc_psum_tensor("pD", [128, 512], F32)
    pK = nc.alloc_psum_tensor("pK", [128, 512], F32)
    pL = nc.alloc_psum_tensor("pL", [128, 512], F32)
    pT = nc.alloc_psum_tensor("pT", [128, 1024], BF16)

    wgrp = wstream[:, 0:16384].rearrange("p (kk n) -> p kk n", n=512)
    xldA = wstream[:, 0:8192].bitcast(F32)
    xldB = wstream[:, 8192:16384].bitcast(F32)
    xnorm = wstream[:, 16384:20480]
    pk_v = wstream[:, 0:8192].bitcast(I32)
    sh_v = wstream[:, 8192:16384].bitcast(I32)
    wrow_v = wstream[:, 16384:20480]
    biasq = wstream[0:1, 16384:20480]
    gate_v = wstream[:, 0:8192]
    attn_v = wstream[:, 8192:16384]
    s5s = wstream[:, 16384:20480]
    smT2 = s5s[:, 0:1024]
    qf2 = s5s[:, 1024:1536]
    qs2 = s5s[:, 1536:2048]
    kf2 = s5s[:, 2048:2560]
    kt2 = s5s[:, 2560:3072]
    qt2 = s5s[:, 3072:3584]
    srep_v = kvst[:, 0:4096]

    s_x0 = [scratch[:, 0:512], scratch[:, 512:1024]]
    s_st = [scratch[:, 1024:1536], scratch[:, 1536:2048]]
    s_sq = scratch[:, 2048:2560]

    bar = Ct(nc, "bar")
    wq = Ct(nc, "wq")
    wqb = Ct(nc, "wqb")
    wgq = Ct(nc, "wgq")
    xq0 = Ct(nc, "xq0")
    xq1 = Ct(nc, "xq1")
    xqp = [xq0, xq1]
    gp0 = Ct(nc, "gp0")
    gp1 = Ct(nc, "gp1")
    gpp = [gp0, gp1]
    gq = Ct(nc, "gq")
    cq = Ct(nc, "cq")
    tm = Ct(nc, "tm")
    dv = Ct(nc, "dv")
    pv = Ct(nc, "pv")
    am = Ct(nc, "am")

    blk = nc.Block()
    blk.__enter__()

    def edge(eng, sem):
        sem.inc(None) if False else None

    def vedge(v):
        pass

    def barrier():
        def dma_fence(s):
            for sem in (wq, wqb, wgq, xq0, xq1, gp0, gp1, gq, cq):
                s.wait_ge(sem.h, sem.n)
        blk.sync(dma_fence)
        own = {"tensor": (tm,), "vector": (dv, pv), "scalar": (am,),
               "sync": (), "gpsimd": ()}
        bar.n += 5
        n = bar.n
        for name in ("sync", "tensor", "vector", "scalar", "gpsimd"):
            def f(eng, n=n, name=name):
                for sem in own[name]:
                    eng.wait_ge(sem.h, sem.n)
                eng.nop().then_inc(bar.h, 1)
                eng.wait_ge(bar.h, n)
            getattr(blk, name)(f)

    def ph_consts(s):
        for dst, src in ((kdec_s[:], kdec), (qdec_s[:], qdec), (gam_s[:], gamma),
                         (nrmw_s[:], nrmw), (ident[:], ident_in)):
            cq.inc(s.dma_start(out=dst, in_=src), 16)
    blk.scalar(ph_consts)

    def ph_consts_v(v):
        dv.inc(v.memset(ones1[:], 1.0))
    blk.vector(ph_consts_v)

    deq_dv0 = dv.n
    deq_gp0 = gp0.n
    for cc in range(4):
        def deq_load(s, cc=cc):
            if cc > 0:
                s.wait_ge(dv.h, deq_dv0 + cc * 16)
            wq.inc(s.dma_start(out=pk_v[:], in_=packT[cc * 128:(cc + 1) * 128, :]), 16)
            for g8 in range(8):
                grow = cc * 8 + g8
                wq.inc(s.dma_start(
                    out=srep_v[16 * g8:16 * g8 + 16, :],
                    in_=scaleT[grow:grow + 1, :].broadcast_to((16, DM))), 16)
        blk.sync(deq_load)

        def deq_v(v, cc=cc):
            wq.wait(v)
            for e in range(8):
                dv.inc(v.tensor_scalar(out=sh_v[:], in0=pk_v[:], scalar1=4 * e,
                                       scalar2=28, op0=ALU.logical_shift_left,
                                       op1=ALU.arith_shift_right))
                dv.wait(v)
                if cc * 8 + e >= 1:
                    v.wait_ge(gp0.h, deq_gp0 + (cc * 8 + e) * 16)
                dv.inc(v.scalar_tensor_tensor(out=wrow_v[:], in0=sh_v[:], scalar=1.0,
                                              in1=srep_v[:], op0=ALU.bypass,
                                              op1=ALU.mult))
                dv.wait(v)
        blk.vector(deq_v)

        def deq_spill(g, cc=cc):
            for e in range(8):
                g.wait_ge(dv.h, deq_dv0 + cc * 16 + 2 * e + 2)
                dst = wdt_d.rearrange("(c e) n -> c e n", e=8)[:, e, :]
                gp0.inc(g.dma_start(out=dst[128 * cc:128 * (cc + 1), :], in_=wrow_v[:]), 16)
        blk.gpsimd(deq_spill)

    def kvzero(v):
        gp0.wait(v)
        dv.inc(v.memset(kvst[:], 0.0))
    blk.vector(kvzero)
    barrier()

    def ln_phase(x_src, nch, tok, use_stats):
        am0, tm0, step = am.n, tm.n, (2 if use_stats is None else 1)
        for c in range(nch):
            xl = xldA if c % 2 == 0 else xldB

            def load(g, c=c, xl=xl):
                if c >= 2:
                    g.wait_ge(am.h, am0 + step * (c - 1))
                xqp[c % 2].inc(g.dma_start(out=xl[:], in_=x_src[c * 128:(c + 1) * 128, :]), 16)
            blk.gpsimd(load)

            if use_stats is None:
                def stats(v, c=c, xl=xl):
                    xqp[c % 2].wait(v)
                    ins = None
                    for i in range(8):
                        ins = v.bn_stats(bn[:, 6 * i:6 * i + 6], xl[:, 512 * i:512 * (i + 1)])
                    dv.inc(ins)
                    dv.wait(v)
                    dv.inc(v.bn_aggr(bnag[:], bn[:]))
                    dv.wait(v)
                    dv.inc(v.tensor_scalar(out=t1[:], in0=bnag[:, 1:2], scalar1=EPS,
                                           scalar2=None, op0=ALU.add))
                blk.vector(stats)

                def st_a(s):
                    dv.wait(s)
                    am.inc(s.activation(out=t4[:], in_=t1[:], func=AF.Sqrt))
                blk.scalar(st_a)

                def st_v(v):
                    am.wait(v)
                    dv.inc(v.reciprocal(t2[:], t4[:]))
                    dv.wait(v)
                    dv.inc(v.scalar_tensor_tensor(out=t3[:], in0=bnag[:, 0:1],
                                                  scalar=-1.0, in1=t2[:],
                                                  op0=ALU.mult, op1=ALU.mult))
                blk.vector(st_v)
                sc_ap, bi_ap = t2[:], t3[:]
            else:
                lnr_a, lnb_a = use_stats
                sc_ap, bi_ap = lnr_a[:, c:c + 1], lnb_a[:, c:c + 1]

            def norm(s, c=c, xl=xl, sc=sc_ap, bi=bi_ap, us=use_stats):
                if us is None:
                    dv.wait(s)
                else:
                    xqp[c % 2].wait(s)
                if c >= 1:
                    s.wait_ge(tm.h, tm0 + c * 8)
                am.inc(s.activation(out=xnorm[:], in_=xl[:], func=AF.Identity,
                                    bias=bi, scale=sc))
            blk.scalar(norm)

            for kb in range(8):
                half = kb % 2

                def trans(te, c=c, kb=kb, half=half):
                    if kb == 0:
                        am.wait(te)
                    dv.wait(te)
                    ins = None
                    for k8 in range(4):
                        k = kb * 4 + k8
                        ins = te.transpose(
                            pT[:, half * 512 + k8 * 128: half * 512 + (k8 + 1) * 128],
                            xnorm[:, k * 128:(k + 1) * 128], ident[:])
                    tm.inc(ins)
                blk.tensor(trans)

                def evac(v, c=c, kb=kb, half=half):
                    tm.wait(v)
                    ins = None
                    for k8 in range(4):
                        k = kb * 4 + k8
                        ins = v.tensor_copy(
                            arena[:, k * tok + c * 128: k * tok + (c + 1) * 128],
                            pT[:, half * 512 + k8 * 128: half * 512 + (k8 + 1) * 128])
                    dv.inc(ins)
                blk.vector(evac)
        barrier()

    def mm_stage(w_src, n_ft, nch, tok, evac_fn, pfree_sem, bias=False,
                 wscale=False, weng="sync"):
        tm0 = tm.n
        pf_base = pfree_sem.n
        if bias:
            def ldb(s):
                cq.inc(s.dma_start(out=biasq[0:1, 0:DM], in_=fbias_bf), 16)
            blk.scalar(ldb)

        for ft in range(n_ft):
            tmb = tm.n

            wsem = wq if weng == "sync" else wgq

            def wload(eng, ft=ft, tmb=tmb, wsem=wsem):
                if ft > 0:
                    eng.wait_ge(tm.h, tmb)
                src = w_src.rearrange("(kk p) n -> p kk n", p=128)[:, :, ft * 512:(ft + 1) * 512]
                wsem.inc(eng.dma_start(out=wgrp[:, 0:16, :], in_=src[:, 0:16, :]), 16)
                wsem.inc(eng.dma_start(out=wgrp[:, 16:32, :], in_=src[:, 16:32, :]), 16)
            getattr(blk, weng)(wload)

            if wscale:
                def wsc(v, ft=ft, wsem=wsem):
                    wsem.wait(v)
                    ins = None
                    for k in range(KC):
                        ins = v.tensor_scalar(out=wgrp[:, k, :], in0=wgrp[:, k, :],
                                              scalar1=nrmw_s[:, k:k + 1],
                                              scalar2=None, op0=ALU.mult)
                    dv.inc(ins)
                blk.vector(wsc)

            def mms(te, ft=ft, wsem=wsem):
                if wscale:
                    dv.wait(te)
                else:
                    wsem.wait(te)
                if ft == 0:
                    cq.wait(te)
                for c in range(nch):
                    idx = ft * nch + c
                    ps = pA if idx % 2 == 0 else pB
                    if idx >= 2:
                        te.wait_ge(pfree_sem.h, pf_base + idx - 1)
                    ins = None
                    for k in range(KC):
                        ins = te.matmul(ps[:],
                                        arena[:, k * tok + c * 128:k * tok + (c + 1) * 128],
                                        wgrp[:, k, :], start=(k == 0),
                                        stop=(k == KC - 1 and not bias))
                    if bias:
                        ins = te.matmul(ps[:], ones1[:],
                                        biasq[0:1, ft * 512:(ft + 1) * 512],
                                        start=False, stop=True)
                    tm.inc(ins)
            blk.tensor(mms)

            for c in range(nch):
                idx = ft * nch + c
                ps = pA if idx % 2 == 0 else pB
                evac_fn(ft, c, idx, ps, tm0)
        barrier()

    BASES = {}

    def set_bases():
        BASES.update(pv=pv.n, gp0=gp0.n, gp1=gp1.n, am=am.n)

    def make_ffn_evac(x0_src, out_dram, stats):
        def evac(ft, c, idx, ps, tm0):
            sx, st_ = s_x0[idx % 2], s_st[idx % 2]
            b_pv, b_am = BASES["pv"], BASES["am"]
            b_gpp = BASES["gp0"] if idx % 2 == 0 else BASES["gp1"]

            def xload(g, ft=ft, c=c, idx=idx, sx=sx):
                if idx >= 2:
                    g.wait_ge(pv.h, b_pv + idx - 1)
                xqp[idx % 2].inc(g.dma_start(
                    out=sx, in_=x0_src[c * 128:(c + 1) * 128, ft * 512:(ft + 1) * 512]), 16)
            blk.gpsimd(xload)

            def ev(v, ft=ft, c=c, idx=idx, ps=ps, sx=sx, st_=st_, tm0=tm0, b_gpp=b_gpp):
                v.wait_ge(tm.h, tm0 + idx + 1)
                xqp[idx % 2].wait(v)
                if idx >= 2:
                    v.wait_ge(gpp[idx % 2].h, b_gpp + 16 * (idx // 2))
                    if stats:
                        v.wait_ge(am.h, b_am + (idx - 1))
                if DBG_RAW:
                    pv.inc(v.tensor_copy(st_, ps[:]))
                else:
                    pv.inc(v.scalar_tensor_tensor(out=st_, in0=ps[:], scalar=0.5, in1=sx,
                                                  op0=ALU.mult, op1=ALU.add))
                if stats:
                    pv.wait(v)
                    dv.inc(v.reduce_sum(sqcol[:, 0:1], st_, axis=AX.X))
                    dv.wait(v)
                    dv.inc(v.tensor_tensor(lnsum[:, c:c + 1], lnsum[:, c:c + 1],
                                           sqcol[:, 0:1], ALU.add))
            blk.vector(ev)

            if stats:
                def sqa(s, st_=st_):
                    pv.wait(s)
                    dv.wait(s)
                    am.inc(s.activation(out=s_sq, in_=st_, func=AF.Square,
                                        accum_out=sqcol[:, 1:2]))
                blk.scalar(sqa)

                def sqv(v, c=c):
                    am.wait(v)
                    dv.inc(v.tensor_tensor(lnsq[:, c:c + 1], lnsq[:, c:c + 1],
                                           sqcol[:, 1:2], ALU.add))
                blk.vector(sqv)

            def spill(g, ft=ft, c=c, idx=idx, st_=st_):
                pv.wait(g)
                gpp[idx % 2].inc(g.dma_start(
                    out=out_dram[c * 128:(c + 1) * 128, ft * 512:(ft + 1) * 512],
                    in_=st_), 16)
            blk.gpsimd(spill)
        return evac

    def make_act_evac(func, spec_fn):
        def evac(ft, c, idx, ps, tm0):
            st_ = s_st[idx % 2]
            dram, colofs = spec_fn(ft)
            b_gpp = BASES["gp0"] if idx % 2 == 0 else BASES["gp1"]

            def ev(s, ft=ft, idx=idx, ps=ps, st_=st_, tm0=tm0, b_gpp=b_gpp):
                s.wait_ge(tm.h, tm0 + idx + 1)
                if idx >= 2:
                    s.wait_ge(gpp[idx % 2].h, b_gpp + 16 * (idx // 2))
                am.inc(s.activation(out=st_.bitcast(BF16)[:, 0:512], in_=ps[:],
                                    func=func))
            blk.scalar(ev)

            def spill(g, c=c, idx=idx, st_=st_, dram=dram, colofs=colofs):
                am.wait(g)
                gpp[idx % 2].inc(g.dma_start(
                    out=dram[c * 128:(c + 1) * 128, colofs:colofs + 512],
                    in_=st_.bitcast(BF16)[:, 0:512]), 16)
            blk.gpsimd(spill)
        return evac

    def make_oproj_evac():
        def evac(ft, c, idx, ps, tm0):
            sx, st_ = s_x0[idx % 2], s_st[idx % 2]
            b_pv, b_am = BASES["pv"], BASES["am"]
            b_gpp = BASES["gp0"] if idx % 2 == 0 else BASES["gp1"]

            def xload(g, ft=ft, c=c, idx=idx, sx=sx):
                if idx >= 2:
                    g.wait_ge(pv.h, b_pv + idx - 1)
                xqp[idx % 2].inc(g.dma_start(
                    out=sx, in_=x1_d[c * 128:(c + 1) * 128, ft * 512:(ft + 1) * 512]), 16)
            blk.gpsimd(xload)

            def ev(v, ft=ft, c=c, idx=idx, ps=ps, sx=sx, st_=st_, tm0=tm0, b_gpp=b_gpp):
                v.wait_ge(tm.h, tm0 + idx + 1)
                xqp[idx % 2].wait(v)
                if idx >= 2:
                    v.wait_ge(gpp[idx % 2].h, b_gpp + 16 * (idx // 2))
                    v.wait_ge(am.h, b_am + (idx - 1))
                pv.inc(v.tensor_tensor(st_, ps[:], sx, ALU.add))
                pv.wait(v)
                dv.inc(v.reduce_sum(sqcol[:, 0:1], st_, axis=AX.X))
                dv.wait(v)
                dv.inc(v.tensor_tensor(lnsum[:, c:c + 1], lnsum[:, c:c + 1],
                                       sqcol[:, 0:1], ALU.add))
            blk.vector(ev)

            def sqa(s, st_=st_):
                pv.wait(s)
                dv.wait(s)
                am.inc(s.activation(out=s_sq, in_=st_, func=AF.Square,
                                    accum_out=sqcol[:, 1:2]))
            blk.scalar(sqa)

            def sqv(v, c=c):
                am.wait(v)
                dv.inc(v.tensor_tensor(lnsq[:, c:c + 1], lnsq[:, c:c + 1],
                                       sqcol[:, 1:2], ALU.add))
            blk.vector(sqv)

            def spill(g, ft=ft, c=c, idx=idx, st_=st_):
                pv.wait(g)
                gpp[idx % 2].inc(g.dma_start(
                    out=x2_d[c * 128:(c + 1) * 128, ft * 512:(ft + 1) * 512],
                    in_=st_), 16)
            blk.gpsimd(spill)
        return evac

    def make_out_evac():
        def evac(ft, c, idx, ps, tm0):
            sx, st_ = s_x0[idx % 2], s_st[idx % 2]
            b_pv = BASES["pv"]
            b_gpp = BASES["gp0"] if idx % 2 == 0 else BASES["gp1"]

            def xload(g, ft=ft, c=c, idx=idx, sx=sx):
                if idx >= 2:
                    g.wait_ge(pv.h, b_pv + idx - 1)
                xqp[idx % 2].inc(g.dma_start(
                    out=sx, in_=x2_d[c * 128:(c + 1) * 128, ft * 512:(ft + 1) * 512]), 16)
            blk.gpsimd(xload)

            def ev(v, ft=ft, c=c, idx=idx, ps=ps, sx=sx, st_=st_, tm0=tm0, b_gpp=b_gpp):
                v.wait_ge(tm.h, tm0 + idx + 1)
                xqp[idx % 2].wait(v)
                if idx >= 2:
                    v.wait_ge(gpp[idx % 2].h, b_gpp + 16 * (idx // 2))
                pv.inc(v.scalar_tensor_tensor(out=st_, in0=ps[:], scalar=0.5, in1=sx,
                                              op0=ALU.mult, op1=ALU.add))
            blk.vector(ev)

            def spill(g, ft=ft, c=c, idx=idx, st_=st_):
                pv.wait(g)
                gpp[idx % 2].inc(g.dma_start(
                    out=y_out[c * 128:(c + 1) * 128, ft * 512:(ft + 1) * 512],
                    in_=st_), 16)
            blk.gpsimd(spill)
        return evac

    def zero_lnstats():
        def z(v):
            v.memset(lnsum[:], 0.0)
            dv.inc(v.memset(lnsq[:], 0.0))
        blk.vector(z)

    def ln_finalize(nch):
        def fin(v):
            dv.inc(v.tensor_scalar(out=lntmp[:, 0:nch], in0=lnsum[:, 0:nch],
                                   scalar1=1.0 / DM, scalar2=None, op0=ALU.mult))
            dv.inc(v.tensor_scalar(out=lnr[:, 0:nch], in0=lnsq[:, 0:nch],
                                   scalar1=1.0 / DM, scalar2=None, op0=ALU.mult))
            dv.wait(v)
            dv.inc(v.tensor_tensor(lnb[:, 0:nch], lntmp[:, 0:nch], lntmp[:, 0:nch],
                                   ALU.mult))
            dv.wait(v)
            dv.inc(v.tensor_tensor(lnsq[:, 0:nch], lnr[:, 0:nch], lnb[:, 0:nch],
                                   ALU.subtract))
            dv.wait(v)
            dv.inc(v.tensor_scalar(out=lnsq[:, 0:nch], in0=lnsq[:, 0:nch], scalar1=EPS,
                                   scalar2=None, op0=ALU.add))
        blk.vector(fin)

        def fin_a(s):
            dv.wait(s)
            am.inc(s.activation(out=lnr[:, 0:nch], in_=lnsq[:, 0:nch], func=AF.Sqrt))
        blk.scalar(fin_a)

        def fin_v(v):
            am.wait(v)
            dv.inc(v.reciprocal(lnsq[:, 0:nch], lnr[:, 0:nch]))
            dv.wait(v)
            dv.inc(v.tensor_copy(lnr[:, 0:nch], lnsq[:, 0:nch]))
            dv.wait(v)
            dv.inc(v.scalar_tensor_tensor(out=lnb[:, 0:nch], in0=lntmp[:, 0:nch],
                                          scalar=-1.0, in1=lnr[:, 0:nch],
                                          op0=ALU.mult, op1=ALU.mult))
        blk.vector(fin_v)
        barrier()

    def kv_accum_scan(ksrc, vsrc, nbl):
        sc_dv = {}
        for b in range(nbl):
            def bload(s, b=b):
                if b >= 1:
                    s.wait_ge(dv.h, sc_dv[b - 1])
                srck = ksrc.rearrange("(bb ct p) n -> bb p ct n", ct=2, p=128)
                dstk = gate_v.rearrange("p (ct n) -> p ct n", n=4096)
                wq.inc(s.dma_start(out=dstk, in_=srck[b]), 16)
                srcv = vsrc.rearrange("(bb ct p) n -> bb p ct n", ct=2, p=128)
                dstv = attn_v.rearrange("p (ct n) -> p ct n", n=4096)
                wq.inc(s.dma_start(out=dstv, in_=srcv[b]), 16)
            blk.sync(bload)

            for pp in range(H // 2):
                hs = (2 * pp, 2 * pp + 1)

                def ktv(v, pp=pp, hs=hs):
                    if pp == 0:
                        wq.wait(v)
                    tm.wait(v)
                    ins = None
                    for j, h in enumerate(hs):
                        for ct in range(2):
                            ins = v.tensor_scalar(
                                out=kt2[:, j * 256 + ct * 128:j * 256 + (ct + 1) * 128],
                                in0=gate_v[:, ct * 4096 + h * 128: ct * 4096 + (h + 1) * 128],
                                scalar1=kdec_s[:, 2 * h + ct:2 * h + ct + 1],
                                scalar2=None, op0=ALU.mult)
                    dv.inc(ins)
                blk.vector(ktv)

                def kvmm(te, pp=pp, hs=hs):
                    dv.wait(te)
                    ins = None
                    for j in range(2):
                        h = hs[j]
                        for ct in range(2):
                            ins = te.matmul(pK[:, j * 128:(j + 1) * 128],
                                            kt2[:, j * 256 + ct * 128:j * 256 + (ct + 1) * 128],
                                            attn_v[:, ct * 4096 + h * 128: ct * 4096 + (h + 1) * 128],
                                            start=(ct == 0 and j == 0),
                                            stop=(ct == 1 and j == 1))
                    tm.inc(ins)
                blk.tensor(kvmm)

                def kvup(v, pp=pp, hs=hs):
                    tm.wait(v)
                    for j in range(2):
                        h = hs[j]
                        dv.inc(v.scalar_tensor_tensor(out=kvst[:, h * 128:(h + 1) * 128],
                                                      in0=kvst[:, h * 128:(h + 1) * 128],
                                                      scalar=bdecs[h], in1=pK[:, j * 128:(j + 1) * 128],
                                                      op0=ALU.mult, op1=ALU.add))
                        dv.wait(v)
                blk.vector(kvup)
            sc_dv[b] = dv.n
        barrier()

    UPTO = cfg.get("upto", 99)
    STEP = [0]

    def take():
        STEP[0] += 1
        return STEP[0] <= UPTO

    if NCP and take():
        ln_phase(x_pref, NCP, TOKP, None)
        zero_lnstats()
        set_bases()
        mm_stage(wdt_d, 8, NCP, TOKP, make_ffn_evac(x_pref, x1p_d, True), pv,
                 bias=True, weng="sync")
        ln_finalize(NCP)
        ln_phase(x1p_d, NCP, TOKP, (lnr, lnb))
        set_bases()

        def kv_spec(ft):
            return (kp_d, ft * 512) if ft < 8 else (vp_d, (ft - 8) * 512)
        mm_stage(wqkvT[:, 4096:12288], 16, NCP, TOKP,
                 make_act_evac(AF.Silu, kv_spec), am, weng="gpsimd")
        kv_accum_scan(kp_d, vp_d, NBP)

        def gmask(v):
            dv.inc(v.tensor_scalar(out=kvst[:], in0=kvst[:], scalar1=gam_s[:],
                                   scalar2=None, op0=ALU.mult))
        blk.vector(gmask)

    def kvb_init(v):
        dv.wait(v)
        dv.inc(v.tensor_copy(kvb[:], kvst[:]))
    blk.vector(kvb_init)
    barrier()

    if take():
        ln_phase(x_own, NCO, TOKO, None)
    if take():
        zero_lnstats()
        set_bases()
        mm_stage(wdt_d, 8, NCO, TOKO, make_ffn_evac(x_own, x1_d, True), pv,
                 bias=not cfg.get("dbg_raw"), weng="sync")
        ln_finalize(NCO)
    if take():
        ln_phase(x1_d, NCO, TOKO, (lnr, lnb))
    if take():
        set_bases()

        def qkv_spec(ft):
            if ft < 8:
                return q_d, ft * 512
            if ft < 16:
                return k_d, (ft - 8) * 512
            return v_d, (ft - 16) * 512
        mm_stage(wqkvT, 24, NCO, TOKO, make_act_evac(AF.Silu, qkv_spec), am,
                 weng="gpsimd")
    if take():
        set_bases()

        def gate_spec(ft):
            return g_d, ft * 512
        mm_stage(wgT, 8, NCO, TOKO, make_act_evac(AF.Sigmoid, gate_spec), am,
                 weng="gpsimd")

    s5_on = take()
    if s5_on:
        def s5_init(v):
            dv.inc(v.memset(rsum[:], 0.0))
        blk.vector(s5_init)

        def s5_dd(s):
            src = ddc.rearrange("(hk p) n -> p hk n", p=128)
            dst = arena[:, 0:16384].rearrange("p (hk n) -> p hk n", n=256)
            wq.inc(s.dma_start(out=dst, in_=src), 16)
        blk.sync(s5_dd)

        QOFS, KOFS, VOFS = 16384, 32768, 49152
        s5_tm_end = {}
        s5_gsp = {}
        for b in range(NBO):
            qo = QOFS + (b % 2) * 8192
            ko = KOFS + (b % 2) * 8192
            vo = VOFS + (b % 2) * 8192

            def bload(s, b=b, qo=qo, ko=ko, vo=vo):
                if b >= 2:
                    s.wait_ge(tm.h, s5_tm_end[b - 2])
                for dram, ofs in ((q_d, qo), (k_d, ko), (v_d, vo)):
                    srcd = dram.rearrange("(bb ct p) n -> bb p ct n", ct=2, p=128)
                    dstd = arena[:, ofs:ofs + 8192].rearrange("p (ct n) -> p ct n", n=4096)
                    wq.inc(s.dma_start(out=dstd, in_=srcd[b]), 16)
            blk.sync(bload)

            def gload(g, b=b):
                if b >= 1:
                    g.wait_ge(gq.h, s5_gsp[b - 1])
                srcd = g_d.rearrange("(bb ct p) n -> bb p ct n", ct=2, p=128)
                dstg = gate_v.rearrange("p (ct n) -> p ct n", n=4096)
                gq.inc(g.dma_start(out=dstg, in_=srcd[b]), 16)
            blk.gpsimd(gload)

            for pp in range(H // 2):
                hs = (2 * pp, 2 * pp + 1)

                def pre(v, pp=pp, hs=hs, b=b, qo=qo, ko=ko):
                    if pp == 0:
                        wq.wait(v)
                        am.wait(v)
                    tm.wait(v)
                    ins = None
                    for j, h in enumerate(hs):
                        for ct in range(2):
                            v.tensor_scalar(out=qt2[:, j * 256 + ct * 128:j * 256 + (ct + 1) * 128],
                                            in0=arena[:, qo + ct * 4096 + h * 128: qo + ct * 4096 + (h + 1) * 128],
                                            scalar1=qdec_s[:, 2 * h + ct:2 * h + ct + 1],
                                            scalar2=None, op0=ALU.mult)
                            ins = v.tensor_scalar(out=kt2[:, j * 256 + ct * 128:j * 256 + (ct + 1) * 128],
                                                  in0=arena[:, ko + ct * 4096 + h * 128: ko + ct * 4096 + (h + 1) * 128],
                                                  scalar1=kdec_s[:, 2 * h + ct:2 * h + ct + 1],
                                                  scalar2=None, op0=ALU.mult)
                    dv.inc(ins)
                blk.vector(pre)

                for j in range(2):
                    def trj(te, pp=pp, j=j, h=hs[j], qo=qo, ko=ko):
                        dv.wait(te)
                        for ct in range(2):
                            te.transpose(pT[:, ct * 128:(ct + 1) * 128],
                                         arena[:, qo + ct * 4096 + h * 128: qo + ct * 4096 + (h + 1) * 128],
                                         ident[:])
                            te.transpose(pT[:, 256 + ct * 128:256 + (ct + 1) * 128],
                                         qt2[:, j * 256 + ct * 128:j * 256 + (ct + 1) * 128],
                                         ident[:])
                            ins = te.transpose(pT[:, 512 + ct * 128:512 + (ct + 1) * 128],
                                               arena[:, ko + ct * 4096 + h * 128: ko + ct * 4096 + (h + 1) * 128],
                                               ident[:])
                        tm.inc(ins)
                    blk.tensor(trj)

                    def trevj(v, pp=pp, j=j):
                        tm.wait(v)
                        v.tensor_copy(qf2[:, j * 256:(j + 1) * 256], pT[:, 0:256])
                        v.tensor_copy(qs2[:, j * 256:(j + 1) * 256], pT[:, 256:512])
                        dv.inc(v.tensor_copy(kf2[:, j * 256:(j + 1) * 256], pT[:, 512:768]))
                    blk.vector(trevj)

                def sco(te, pp=pp):
                    dv.wait(te)
                    ins = None
                    for j, pS in ((0, pS0), (1, pA)):
                        for ktc in range(2):
                            ins = te.matmul(pS[:, ktc * 256:(ktc + 1) * 256],
                                            kf2[:, j * 256 + ktc * 128:j * 256 + (ktc + 1) * 128],
                                            qf2[:, j * 256:(j + 1) * 256],
                                            start=True, stop=True)
                    tm.inc(ins)
                blk.tensor(sco)

                def msk(v, pp=pp, hs=hs):
                    tm.wait(v)
                    ins = None
                    for j, pS in ((0, pS0), (1, pA)):
                        h = hs[j]
                        for ktc in range(2):
                            ins = v.tensor_tensor(
                                smT2[:, j * 512 + ktc * 256:j * 512 + (ktc + 1) * 256],
                                pS[:, ktc * 256:(ktc + 1) * 256],
                                arena[:, h * 512 + ktc * 256: h * 512 + (ktc + 1) * 256],
                                ALU.mult)
                    dv.inc(ins)
                blk.vector(msk)

                def amm(te, pp=pp, hs=hs, vo=vo):
                    dv.wait(te)
                    am.wait(te)
                    for j, pat in ((0, pC), (1, pD)):
                        h = hs[j]
                        te.matmul(pat[:, 0:128], smT2[:, j * 512:j * 512 + 128],
                                  arena[:, vo + h * 128: vo + (h + 1) * 128],
                                  start=True, stop=False)
                        for (ct, ktc) in ((0, 1), (1, 0), (1, 1)):
                            te.matmul(pat[:, ct * 128:(ct + 1) * 128],
                                      smT2[:, j * 512 + ktc * 256 + ct * 128:j * 512 + ktc * 256 + (ct + 1) * 128],
                                      arena[:, vo + ktc * 4096 + h * 128: vo + ktc * 4096 + (h + 1) * 128],
                                      start=False, stop=False)
                        te.matmul(pat[:, 0:128], qs2[:, j * 256:j * 256 + 128],
                                  kvb[:, h * 128:(h + 1) * 128], start=False, stop=False)
                        te.matmul(pat[:, 128:256], qs2[:, j * 256 + 128:j * 256 + 256],
                                  kvb[:, h * 128:(h + 1) * 128], start=False, stop=True)
                    ins = None
                    for j in range(2):
                        h = hs[j]
                        for ct in range(2):
                            ins = te.matmul(pK[:, j * 128:(j + 1) * 128],
                                            kt2[:, j * 256 + ct * 128:j * 256 + (ct + 1) * 128],
                                            arena[:, vo + ct * 4096 + h * 128: vo + ct * 4096 + (h + 1) * 128],
                                            start=(ct == 0 and j == 0),
                                            stop=(ct == 1 and j == 1))
                    tm.inc(ins)
                blk.tensor(amm)

                def aev(s, pp=pp, hs=hs):
                    tm.wait(s)
                    am.wait(s)
                    ins = None
                    for j, pat in ((0, pC), (1, pD)):
                        h = hs[j]
                        for ct in range(2):
                            s.activation(out=attn_v[:, ct * 4096 + h * 128: ct * 4096 + (h + 1) * 128],
                                         in_=pat[:, ct * 128:(ct + 1) * 128], func=AF.Identity)
                            ins = s.activation(
                                out=s_sq.bitcast(BF16)[:, (j * 2 + ct) * 256:(j * 2 + ct) * 256 + 256].bitcast(F32),
                                in_=pat[:, ct * 128:(ct + 1) * 128],
                                func=AF.Square, accum_out=sqcol[:, j * 2 + ct:j * 2 + ct + 1])
                    am.inc(ins)
                blk.scalar(aev)

                def fin5(v, pp=pp, hs=hs, b=b):
                    am.wait(v)
                    dv.wait(v)
                    dv.inc(v.tensor_tensor(rsum[:, b * 2:b * 2 + 2],
                                           rsum[:, b * 2:b * 2 + 2],
                                           sqcol[:, 0:2], ALU.add))
                    dv.wait(v)
                    dv.inc(v.tensor_tensor(rsum[:, b * 2:b * 2 + 2],
                                           rsum[:, b * 2:b * 2 + 2],
                                           sqcol[:, 2:4], ALU.add))
                    dv.wait(v)
                    ins = None
                    for j in range(2):
                        h = hs[j]
                        dv.inc(v.scalar_tensor_tensor(out=kvst[:, h * 128:(h + 1) * 128],
                                                      in0=kvst[:, h * 128:(h + 1) * 128],
                                                      scalar=bdecs[h], in1=pK[:, j * 128:(j + 1) * 128],
                                                      op0=ALU.mult, op1=ALU.add))
                        dv.wait(v)
                        dv.inc(v.tensor_copy(kvb[:, h * 128:(h + 1) * 128],
                                             kvst[:, h * 128:(h + 1) * 128]))
                        dv.wait(v)
                    ins = None
                blk.vector(fin5)
            s5_tm_end[b] = tm.n

            def rstd(v, b=b):
                dv.wait(v)
                v.tensor_scalar(out=t1[:], in0=rsum[:, 2 * b:2 * b + 1], scalar1=1.0 / DM,
                                scalar2=EPS, op0=ALU.mult, op1=ALU.add)
                dv.inc(v.tensor_scalar(out=t2[:], in0=rsum[:, 2 * b + 1:2 * b + 2],
                                       scalar1=1.0 / DM, scalar2=EPS, op0=ALU.mult,
                                       op1=ALU.add))
            blk.vector(rstd)

            def rstd_a(s):
                dv.wait(s)
                s.activation(out=t3[:], in_=t1[:], func=AF.Sqrt)
                am.inc(s.activation(out=t4[:], in_=t2[:], func=AF.Sqrt))
            blk.scalar(rstd_a)

            def gmul(v, b=b):
                am.wait(v)
                gq.wait(v)
                v.reciprocal(t5[:], t3[:])
                dv.inc(v.reciprocal(t6[:], t4[:]))
                dv.wait(v)
                for ct in range(2):
                    rr = t5 if ct == 0 else t6
                    dv.inc(v.tensor_scalar(out=attn_v[:, ct * 4096:(ct + 1) * 4096],
                                           in0=attn_v[:, ct * 4096:(ct + 1) * 4096],
                                           scalar1=rr[:], scalar2=None, op0=ALU.mult))
                    dv.wait(v)
                    dv.inc(v.tensor_tensor(attn_v[:, ct * 4096:(ct + 1) * 4096],
                                           attn_v[:, ct * 4096:(ct + 1) * 4096],
                                           gate_v[:, ct * 4096:(ct + 1) * 4096], ALU.mult))
                    dv.wait(v)
            blk.vector(gmul)

            def gspill(g, b=b):
                dv.wait(g)
                dst = gated_d.rearrange("(bb ct p) n -> bb p ct n", ct=2, p=128)
                srcv = attn_v.rearrange("p (ct n) -> p ct n", n=4096)
                gq.inc(g.dma_start(out=dst[b], in_=srcv), 16)
            blk.gpsimd(gspill)
            s5_gsp[b] = gq.n

            def wfence(s, b=b):
                s.wait_ge(gq.h, s5_gsp[b])
            blk.scalar(wfence)
        barrier()

    s6p_on = take()
    s6_tm0 = tm.n
    for c in (range(NCO) if s6p_on else []):
        xl = xldA if c % 2 == 0 else xldB

        def gl(g, c=c, xl=xl):
            if c >= 2:
                g.wait_ge(tm.h, s6_tm0 + (c - 1) * 8)
            xqp[c % 2].inc(g.dma_start(out=xl.bitcast(BF16)[:, 0:4096],
                                       in_=gated_d[c * 128:(c + 1) * 128, :]), 16)
        blk.gpsimd(gl)

        for kb in range(8):
            half = kb % 2

            def gt(te, c=c, kb=kb, half=half, xl=xl):
                if kb == 0:
                    xqp[c % 2].wait(te)
                dv.wait(te)
                ins = None
                for k8 in range(4):
                    k = kb * 4 + k8
                    ins = te.transpose(
                        pT[:, half * 512 + k8 * 128: half * 512 + (k8 + 1) * 128],
                        xl.bitcast(BF16)[:, k * 128:(k + 1) * 128], ident[:])
                tm.inc(ins)
            blk.tensor(gt)

            def ge(v, c=c, kb=kb, half=half):
                tm.wait(v)
                ins = None
                for k8 in range(4):
                    k = kb * 4 + k8
                    ins = v.tensor_copy(
                        arena[:, k * TOKO + c * 128: k * TOKO + (c + 1) * 128],
                        pT[:, half * 512 + k8 * 128: half * 512 + (k8 + 1) * 128])
                dv.inc(ins)
            blk.vector(ge)
    barrier()

    if take():
        zero_lnstats()
        set_bases()
        mm_stage(woT, 8, NCO, TOKO, make_oproj_evac(), pv, wscale=True,
                 weng="gpsimd")
        ln_finalize(NCO)
    if take():
        ln_phase(x2_d, NCO, TOKO, (lnr, lnb))
    if take():
        set_bases()
        mm_stage(wdt_d, 8, NCO, TOKO, make_out_evac(), pv, bias=True,
                 weng="sync")
    if arena_out is not None:
        def dmpa(g):
            for sem in (gp0, gp1, gq, xq0, xq1):
                g.wait_ge(sem.h, sem.n)
            for r0 in range(0, 65536, 16384):
                gq.inc(g.dma_start(out=arena_out[:, r0:r0 + 16384],
                                   in_=arena[:, r0:r0 + 16384]), 16)
        blk.gpsimd(dmpa)
    for name, out_ap in dump_outs.items():
        src = scr_map[name]

        def dmp(g, src=src, out_ap=out_ap):
            for sem in (gp0, gp1, gq, xq0, xq1):
                g.wait_ge(sem.h, sem.n)
            rows = src.shape[0]
            step = 512
            for r0 in range(0, rows, step):
                r1 = min(rows, r0 + step)
                gq.inc(g.dma_start(out=out_ap[r0:r1, :], in_=src[r0:r1, :]), 16)
        blk.gpsimd(dmp)
    barrier()

    def fin(s):
        for sem in (wq, wqb, wgq, xq0, xq1, gp0, gp1, gq, cq):
            s.wait_ge(sem.h, sem.n)
    blk.sync(fin)

    blk.__exit__(None, None, None)
    return nc


def host_consts():
    qd, kd, dd, bd = decay_consts()
    ddc = np.ascontiguousarray(
        dd.transpose(0, 2, 1).reshape(H * 256, 256)).astype(ml_dtypes.bfloat16)
    kdec = np.zeros((128, 2 * H), np.float32)
    qdec = np.zeros((128, 2 * H), np.float32)
    for h in range(H):
        for ct in range(2):
            kdec[:, 2 * h + ct] = kd[h, ct * 128:(ct + 1) * 128]
            qdec[:, 2 * h + ct] = qd[h, ct * 128:(ct + 1) * 128]
    ident = np.eye(128, dtype=ml_dtypes.bfloat16)
    return ddc, kdec, qdec, ident


def host_weights(inputs):
    qkv_w = np.asarray(inputs["L0_attn_qkv_w"])
    wqkvT = np.ascontiguousarray(
        qkv_w.reshape(H, 3, D, DM).transpose(1, 0, 2, 3).reshape(3 * DM, DM).T)
    wgT = np.ascontiguousarray(np.asarray(inputs["L0_attn_gate_w"]).T)
    woT = np.ascontiguousarray(np.asarray(inputs["L0_attn_out_w"]).T)
    packT = np.ascontiguousarray(np.asarray(inputs["L0_ffn_packed_weight"]).T)
    scaleT = np.ascontiguousarray(np.asarray(inputs["L0_ffn_scale"]).T)
    fbias = np.asarray(inputs["L0_ffn_bias"])
    fbias_bf = np.ascontiguousarray(fbias[None, :]).astype(ml_dtypes.bfloat16)
    nrmw = np.ascontiguousarray(
        np.asarray(inputs["L0_attn_norm_w"]).reshape(KC, 128).T)
    return wqkvT, wgT, woT, packT, scaleT, fbias_bf, nrmw


def make_in_map(inputs, x_own_np, x_pref_np, gamma_val, cfg):
    wqkvT, wgT, woT, packT, scaleT, fbias_bf, nrmw = host_weights(inputs)
    ddc, kdec, qdec, ident = host_consts()
    return {
        "x_own": np.ascontiguousarray(x_own_np),
        "x_pref": np.ascontiguousarray(x_pref_np),
        "gamma": np.full((128, 1), gamma_val, np.float32),
        "wqkvT": wqkvT, "wgT": wgT, "woT": woT, "packT": packT,
        "scaleT": scaleT, "fbias_bf": fbias_bf, "nrmw": nrmw,
        "ddc": ddc, "kdec": kdec, "qdec": qdec, "identin": ident,
    }


_CFG = {"nco": 16, "ncp": 16, "dump": []}
_NC_CACHE = []


def kernel(**inputs):
    hid = np.asarray(inputs["L0_attn_hidden_states"])
    wqkvT, wgT, woT, packT, scaleT, fbias_bf, nrmw = host_weights(inputs)
    ddc, kdec, qdec, ident = host_consts()
    if not _NC_CACHE:
        _NC_CACHE.append(build(_CFG))
    nc = _NC_CACHE[0]
    in_maps = []
    for c in range(8):
        b, half = c // 2, c % 2
        x_own = np.ascontiguousarray(hid[b, half * 2048:(half + 1) * 2048])
        x_pref = np.ascontiguousarray(hid[b, 0:2048]) if half else x_own
        in_maps.append({
            "x_own": x_own, "x_pref": x_pref,
            "gamma": np.full((128, 1), float(half), np.float32),
            "wqkvT": wqkvT, "wgT": wgT, "woT": woT, "packT": packT,
            "scaleT": scaleT, "fbias_bf": fbias_bf, "nrmw": nrmw,
            "ddc": ddc, "kdec": kdec, "qdec": qdec, "identin": ident,
        })
    res = run_bass_kernel_spmd(nc, in_maps, core_ids=list(range(8)))
    out = np.empty((4, 4096, 4096), np.float32)
    for c in range(8):
        b, half = c // 2, c % 2
        out[b, half * 2048:(half + 1) * 2048] = res.results[c]["y"]
    return out


# version 04 (final)
# speedup vs baseline: 1.1097x; 1.1097x over previous
import numpy as np
import ml_dtypes
import concourse.bass as bass
import concourse.mybir as mybir
from concourse.bass_utils import run_bass_kernel_spmd

F32 = mybir.dt.float32
BF16 = mybir.dt.bfloat16
I32 = mybir.dt.int32
AF = mybir.ActivationFunctionType
ALU = mybir.AluOpType
AX = mybir.AxisListType
EPS = 1e-5
H, D, BLKT, DM, KC = 32, 128, 256, 4096, 32


def decay_consts():
    base = 1.0 / 2.0 ** (8.0 / H)
    expo = np.arange(H, dtype=np.float64) + 1.0
    factor = 1.0 - 1 / (32 - 1 + 1e-05) + 1e-05
    slope = base ** expo * factor
    r = np.arange(BLKT, dtype=np.float64) + 1.0
    q_decay = np.exp(-slope[:, None] * r[None, :])
    k_decay = np.exp(-slope[:, None] * (BLKT - r)[None, :])
    diff = r[:, None] - r[None, :]
    diag = slope[:, None, None] * diff[None]
    diag = np.where(diag >= 0, -diag, -np.inf)
    diag_decay = np.exp(diag)
    block_decay = np.exp(-slope * BLKT)
    return q_decay, k_decay, diag_decay, block_decay


class Ct:
    def __init__(self, nc, name):
        self.h = nc.alloc_semaphore(name)
        self.n = 0

    def inc(self, instr, k=1):
        instr.then_inc(self.h, k)
        self.n += k
        return instr

    def wait(self, eng, at=None):
        eng.wait_ge(self.h, self.n if at is None else at)


def build(cfg):
    NCO = cfg.get("nco", 16)
    DBG_RAW = bool(cfg.get("dbg_raw"))
    NCP = cfg.get("ncp", 16)
    dump = cfg.get("dump", [])
    TOKO, TOKP = NCO * 128, NCP * 128
    NBO = TOKO // BLKT
    NBP = TOKP // BLKT if NCP else 0

    _, _, _, bdec_np = decay_consts()
    bdecs = [float(b) for b in bdec_np]

    nc = bass.Bass("TRN2", target_bir_lowering=False, dynamic_dma_scratch_size=16384)

    x_own = nc.dram_tensor("x_own", [TOKO, DM], F32, kind="ExternalInput").ap()
    x_pref = nc.dram_tensor("x_pref", [max(TOKP, 128), DM], F32, kind="ExternalInput").ap()
    gamma = nc.dram_tensor("gamma", [128, 1], F32, kind="ExternalInput").ap()
    wqkvT = nc.dram_tensor("wqkvT", [DM, 12288], F32, kind="ExternalInput").ap()
    wgT = nc.dram_tensor("wgT", [DM, DM], F32, kind="ExternalInput").ap()
    woT = nc.dram_tensor("woT", [DM, DM], F32, kind="ExternalInput").ap()
    packT = nc.dram_tensor("packT", [512, DM], I32, kind="ExternalInput").ap()
    scaleT = nc.dram_tensor("scaleT", [32, DM], F32, kind="ExternalInput").ap()
    nrmw = nc.dram_tensor("nrmw", [128, KC], F32, kind="ExternalInput").ap()
    ddc = nc.dram_tensor("ddc", [H * 256, 256], BF16, kind="ExternalInput").ap()
    kdec = nc.dram_tensor("kdec", [128, 2 * H], F32, kind="ExternalInput").ap()
    qdec = nc.dram_tensor("qdec", [128, 2 * H], F32, kind="ExternalInput").ap()
    ident_in = nc.dram_tensor("identin", [128, 128], BF16, kind="ExternalInput").ap()
    fbias_bf = nc.dram_tensor("fbias_bf", [1, DM], BF16, kind="ExternalInput").ap()
    y_out = nc.dram_tensor("y", [TOKO, DM], F32, kind="ExternalOutput").ap()

    wdt_d = nc.dram_tensor("wdt_d", [DM, DM], BF16).ap()
    x1_d = nc.dram_tensor("x1_d", [TOKO, DM], F32).ap()
    x2_d = nc.dram_tensor("x2_d", [TOKO, DM], F32).ap()
    q_d = nc.dram_tensor("q_d", [TOKO, DM], BF16).ap()
    k_d = nc.dram_tensor("k_d", [TOKO, DM], BF16).ap()
    v_d = nc.dram_tensor("v_d", [TOKO, DM], BF16).ap()
    g_d = nc.dram_tensor("g_d", [TOKO, DM], BF16).ap()
    gated_d = nc.dram_tensor("gated_d", [TOKO, DM], BF16).ap()
    scr_map = {"x1_d": x1_d, "x2_d": x2_d, "q_d": q_d, "k_d": k_d, "v_d": v_d,
               "g_d": g_d, "gated_d": gated_d, "wdt_d": wdt_d}
    if NCP:
        x1p_d = nc.dram_tensor("x1p_d", [TOKP, DM], F32).ap()
        kp_d = nc.dram_tensor("kp_d", [TOKP, DM], BF16).ap()
        vp_d = nc.dram_tensor("vp_d", [TOKP, DM], BF16).ap()
        scr_map.update({"x1p_d": x1p_d, "kp_d": kp_d, "vp_d": vp_d})
    dump_outs = {}
    for name in dump:
        if name == "arena":
            continue
        src = scr_map[name]
        dump_outs[name] = nc.dram_tensor(
            "dump_" + name, list(src.shape), src.dtype, kind="ExternalOutput").ap()
    arena_out = (nc.dram_tensor("dump_arena", [128, 65536], BF16,
                                kind="ExternalOutput").ap() if "arena" in dump else None)

    arena = nc.alloc_sbuf_tensor("arena", [128, 65536], BF16)
    kvst = nc.alloc_sbuf_tensor("kvst", [128, 4096], F32)
    kvb = nc.alloc_sbuf_tensor("kvb", [128, 4096], BF16)
    wstream = nc.alloc_sbuf_tensor("wstream", [128, 20480], BF16)
    scratch = nc.alloc_sbuf_tensor("scratch", [128, 2560], F32)
    bn = nc.alloc_sbuf_tensor("bn", [128, 48], F32)
    bnag = nc.alloc_sbuf_tensor("bnag", [128, 2], F32)
    t1 = nc.alloc_sbuf_tensor("t1", [128, 1], F32)
    t2 = nc.alloc_sbuf_tensor("t2", [128, 1], F32)
    t3 = nc.alloc_sbuf_tensor("t3", [128, 1], F32)
    t4 = nc.alloc_sbuf_tensor("t4", [128, 1], F32)
    t5 = nc.alloc_sbuf_tensor("t5", [128, 1], F32)
    t6 = nc.alloc_sbuf_tensor("t6", [128, 1], F32)
    lnsum = nc.alloc_sbuf_tensor("lnsum", [128, 16], F32)
    lnsq = nc.alloc_sbuf_tensor("lnsq", [128, 16], F32)
    lnr = nc.alloc_sbuf_tensor("lnr", [128, 16], F32)
    lnb = nc.alloc_sbuf_tensor("lnb", [128, 16], F32)
    lntmp = nc.alloc_sbuf_tensor("lntmp", [128, 16], F32)
    rsum = nc.alloc_sbuf_tensor("rsum", [128, 16], F32)
    sqcol = nc.alloc_sbuf_tensor("sqcol", [128, 4], F32)
    kdec_s = nc.alloc_sbuf_tensor("kdec_s", [128, 2 * H], F32)
    qdec_s = nc.alloc_sbuf_tensor("qdec_s", [128, 2 * H], F32)
    gam_s = nc.alloc_sbuf_tensor("gam_s", [128, 1], F32)
    nrmw_s = nc.alloc_sbuf_tensor("nrmw_s", [128, KC], F32)
    ident = nc.alloc_sbuf_tensor("ident", [128, 128], BF16)
    ones1 = nc.alloc_sbuf_tensor("ones1", [1, 128], BF16)

    pA = nc.alloc_psum_tensor("pA", [128, 512], F32)
    pB = nc.alloc_psum_tensor("pB", [128, 512], F32)
    pS0 = nc.alloc_psum_tensor("pS0", [128, 512], F32)
    pC = nc.alloc_psum_tensor("pC", [128, 512], F32)
    pD = nc.alloc_psum_tensor("pD", [128, 512], F32)
    pK = nc.alloc_psum_tensor("pK", [128, 512], F32)
    pL = nc.alloc_psum_tensor("pL", [128, 512], F32)
    pT = nc.alloc_psum_tensor("pT", [128, 1024], BF16)

    wgrp = wstream[:, 0:16384].rearrange("p (kk n) -> p kk n", n=512)
    xldA = wstream[:, 0:8192].bitcast(F32)
    xldB = wstream[:, 8192:16384].bitcast(F32)
    xnorm = wstream[:, 16384:20480]
    pk_v = wstream[:, 0:8192].bitcast(I32)
    sh_v = wstream[:, 8192:16384].bitcast(I32)
    wrow_v = wstream[:, 16384:20480]
    biasq = wstream[0:1, 16384:20480]
    gate_v = wstream[:, 0:8192]
    attn_v = wstream[:, 8192:16384]
    s5s = wstream[:, 16384:20480]
    kt2 = s5s[:, 2048:2560]
    smT = s5s[:, 0:512]
    qf = s5s[:, 512:768]
    qs = s5s[:, 768:1024]
    kf = s5s[:, 1024:1280]
    kt_ = s5s[:, 1280:1536]
    qt_ = s5s[:, 1536:1792]
    srep_v = kvst[:, 0:4096]

    s_x0 = [scratch[:, 0:512], scratch[:, 512:1024]]
    s_st = [scratch[:, 1024:1536], scratch[:, 1536:2048]]
    s_sq = scratch[:, 2048:2560]

    bar = Ct(nc, "bar")
    wq = Ct(nc, "wq")
    wqb = Ct(nc, "wqb")
    wgq = Ct(nc, "wgq")
    xq0 = Ct(nc, "xq0")
    xq1 = Ct(nc, "xq1")
    xqp = [xq0, xq1]
    gp0 = Ct(nc, "gp0")
    gp1 = Ct(nc, "gp1")
    gpp = [gp0, gp1]
    gq = Ct(nc, "gq")
    cq = Ct(nc, "cq")
    tm = Ct(nc, "tm")
    dv = Ct(nc, "dv")
    pv = Ct(nc, "pv")
    am = Ct(nc, "am")

    blk = nc.Block()
    blk.__enter__()

    def edge(eng, sem):
        sem.inc(None) if False else None

    def vedge(v):
        pass

    def barrier():
        def dma_fence(s):
            for sem in (wq, wqb, wgq, xq0, xq1, gp0, gp1, gq, cq):
                s.wait_ge(sem.h, sem.n)
        blk.sync(dma_fence)
        own = {"tensor": (tm,), "vector": (dv, pv), "scalar": (am,),
               "sync": (), "gpsimd": ()}
        bar.n += 5
        n = bar.n
        for name in ("sync", "tensor", "vector", "scalar", "gpsimd"):
            def f(eng, n=n, name=name):
                for sem in own[name]:
                    eng.wait_ge(sem.h, sem.n)
                eng.nop().then_inc(bar.h, 1)
                eng.wait_ge(bar.h, n)
            getattr(blk, name)(f)

    def ph_consts(s):
        for dst, src in ((kdec_s[:], kdec), (qdec_s[:], qdec), (gam_s[:], gamma),
                         (nrmw_s[:], nrmw), (ident[:], ident_in)):
            cq.inc(s.dma_start(out=dst, in_=src), 16)
    blk.scalar(ph_consts)

    def ph_consts_v(v):
        dv.inc(v.memset(ones1[:], 1.0))
    blk.vector(ph_consts_v)

    deq_dv0 = dv.n
    deq_gp0 = gp0.n
    for cc in range(4):
        def deq_load(s, cc=cc):
            if cc > 0:
                s.wait_ge(dv.h, deq_dv0 + cc * 16)
            wq.inc(s.dma_start(out=pk_v[:], in_=packT[cc * 128:(cc + 1) * 128, :]), 16)
            for g8 in range(8):
                grow = cc * 8 + g8
                wq.inc(s.dma_start(
                    out=srep_v[16 * g8:16 * g8 + 16, :],
                    in_=scaleT[grow:grow + 1, :].broadcast_to((16, DM))), 16)
        blk.sync(deq_load)

        def deq_v(v, cc=cc):
            wq.wait(v)
            for e in range(8):
                dv.inc(v.tensor_scalar(out=sh_v[:], in0=pk_v[:], scalar1=4 * e,
                                       scalar2=28, op0=ALU.logical_shift_left,
                                       op1=ALU.arith_shift_right))
                dv.wait(v)
                if cc * 8 + e >= 1:
                    v.wait_ge(gp0.h, deq_gp0 + (cc * 8 + e) * 16)
                dv.inc(v.scalar_tensor_tensor(out=wrow_v[:], in0=sh_v[:], scalar=1.0,
                                              in1=srep_v[:], op0=ALU.bypass,
                                              op1=ALU.mult))
                dv.wait(v)
        blk.vector(deq_v)

        def deq_spill(g, cc=cc):
            for e in range(8):
                g.wait_ge(dv.h, deq_dv0 + cc * 16 + 2 * e + 2)
                dst = wdt_d.rearrange("(c e) n -> c e n", e=8)[:, e, :]
                gp0.inc(g.dma_start(out=dst[128 * cc:128 * (cc + 1), :], in_=wrow_v[:]), 16)
        blk.gpsimd(deq_spill)

    def kvzero(v):
        gp0.wait(v)
        dv.inc(v.memset(kvst[:], 0.0))
    blk.vector(kvzero)
    barrier()

    def ln_phase(x_src, nch, tok, use_stats):
        am0, tm0, step = am.n, tm.n, (2 if use_stats is None else 1)
        for c in range(nch):
            xl = xldA if c % 2 == 0 else xldB

            def load(g, c=c, xl=xl):
                if c >= 2:
                    g.wait_ge(am.h, am0 + step * (c - 1))
                xqp[c % 2].inc(g.dma_start(out=xl[:], in_=x_src[c * 128:(c + 1) * 128, :]), 16)
            blk.gpsimd(load)

            if use_stats is None:
                def stats(v, c=c, xl=xl):
                    xqp[c % 2].wait(v)
                    ins = None
                    for i in range(8):
                        ins = v.bn_stats(bn[:, 6 * i:6 * i + 6], xl[:, 512 * i:512 * (i + 1)])
                    dv.inc(ins)
                    dv.wait(v)
                    dv.inc(v.bn_aggr(bnag[:], bn[:]))
                    dv.wait(v)
                    dv.inc(v.tensor_scalar(out=t1[:], in0=bnag[:, 1:2], scalar1=EPS,
                                           scalar2=None, op0=ALU.add))
                blk.vector(stats)

                def st_a(s):
                    dv.wait(s)
                    am.inc(s.activation(out=t4[:], in_=t1[:], func=AF.Sqrt))
                blk.scalar(st_a)

                def st_v(v):
                    am.wait(v)
                    dv.inc(v.reciprocal(t2[:], t4[:]))
                    dv.wait(v)
                    dv.inc(v.scalar_tensor_tensor(out=t3[:], in0=bnag[:, 0:1],
                                                  scalar=-1.0, in1=t2[:],
                                                  op0=ALU.mult, op1=ALU.mult))
                blk.vector(st_v)
                sc_ap, bi_ap = t2[:], t3[:]
            else:
                lnr_a, lnb_a = use_stats
                sc_ap, bi_ap = lnr_a[:, c:c + 1], lnb_a[:, c:c + 1]

            def norm(s, c=c, xl=xl, sc=sc_ap, bi=bi_ap, us=use_stats):
                if us is None:
                    dv.wait(s)
                else:
                    xqp[c % 2].wait(s)
                if c >= 1:
                    s.wait_ge(tm.h, tm0 + c * 8)
                am.inc(s.activation(out=xnorm[:], in_=xl[:], func=AF.Identity,
                                    bias=bi, scale=sc))
            blk.scalar(norm)

            for kb in range(8):
                half = kb % 2

                def trans(te, c=c, kb=kb, half=half):
                    if kb == 0:
                        am.wait(te)
                    dv.wait(te)
                    ins = None
                    for k8 in range(4):
                        k = kb * 4 + k8
                        ins = te.transpose(
                            pT[:, half * 512 + k8 * 128: half * 512 + (k8 + 1) * 128],
                            xnorm[:, k * 128:(k + 1) * 128], ident[:])
                    tm.inc(ins)
                blk.tensor(trans)

                def evac(v, c=c, kb=kb, half=half):
                    tm.wait(v)
                    ins = None
                    for k8 in range(4):
                        k = kb * 4 + k8
                        ins = v.tensor_copy(
                            arena[:, k * tok + c * 128: k * tok + (c + 1) * 128],
                            pT[:, half * 512 + k8 * 128: half * 512 + (k8 + 1) * 128])
                    dv.inc(ins)
                blk.vector(evac)
        barrier()

    def mm_stage(w_src, n_ft, nch, tok, evac_fn, pfree_sem, bias=False,
                 wscale=False, weng="sync"):
        tm0 = tm.n
        pf_base = pfree_sem.n
        if bias:
            def ldb(s):
                cq.inc(s.dma_start(out=biasq[0:1, 0:DM], in_=fbias_bf), 16)
            blk.scalar(ldb)

        for ft in range(n_ft):
            tmb = tm.n

            wsem = wq if weng == "sync" else wgq

            def wload(eng, ft=ft, tmb=tmb, wsem=wsem):
                if ft > 0:
                    eng.wait_ge(tm.h, tmb)
                src = w_src.rearrange("(kk p) n -> p kk n", p=128)[:, :, ft * 512:(ft + 1) * 512]
                wsem.inc(eng.dma_start(out=wgrp[:, 0:16, :], in_=src[:, 0:16, :]), 16)
                wsem.inc(eng.dma_start(out=wgrp[:, 16:32, :], in_=src[:, 16:32, :]), 16)
            getattr(blk, weng)(wload)

            if wscale:
                def wsc(v, ft=ft, wsem=wsem):
                    wsem.wait(v)
                    ins = None
                    for k in range(KC):
                        ins = v.tensor_scalar(out=wgrp[:, k, :], in0=wgrp[:, k, :],
                                              scalar1=nrmw_s[:, k:k + 1],
                                              scalar2=None, op0=ALU.mult)
                    dv.inc(ins)
                blk.vector(wsc)

            def mms(te, ft=ft, wsem=wsem):
                if wscale:
                    dv.wait(te)
                else:
                    wsem.wait(te)
                if ft == 0:
                    cq.wait(te)
                for c in range(nch):
                    idx = ft * nch + c
                    ps = pA if idx % 2 == 0 else pB
                    if idx >= 2:
                        te.wait_ge(pfree_sem.h, pf_base + idx - 1)
                    ins = None
                    for k in range(KC):
                        ins = te.matmul(ps[:],
                                        arena[:, k * tok + c * 128:k * tok + (c + 1) * 128],
                                        wgrp[:, k, :], start=(k == 0),
                                        stop=(k == KC - 1 and not bias))
                    if bias:
                        ins = te.matmul(ps[:], ones1[:],
                                        biasq[0:1, ft * 512:(ft + 1) * 512],
                                        start=False, stop=True)
                    tm.inc(ins)
            blk.tensor(mms)

            for c in range(nch):
                idx = ft * nch + c
                ps = pA if idx % 2 == 0 else pB
                evac_fn(ft, c, idx, ps, tm0)
        barrier()

    BASES = {}

    def set_bases():
        BASES.update(pv=pv.n, gp0=gp0.n, gp1=gp1.n, am=am.n)

    def make_ffn_evac(x0_src, out_dram, stats):
        def evac(ft, c, idx, ps, tm0):
            sx, st_ = s_x0[idx % 2], s_st[idx % 2]
            b_pv, b_am = BASES["pv"], BASES["am"]
            b_gpp = BASES["gp0"] if idx % 2 == 0 else BASES["gp1"]

            def xload(g, ft=ft, c=c, idx=idx, sx=sx):
                if idx >= 2:
                    g.wait_ge(pv.h, b_pv + idx - 1)
                xqp[idx % 2].inc(g.dma_start(
                    out=sx, in_=x0_src[c * 128:(c + 1) * 128, ft * 512:(ft + 1) * 512]), 16)
            blk.gpsimd(xload)

            def ev(v, ft=ft, c=c, idx=idx, ps=ps, sx=sx, st_=st_, tm0=tm0, b_gpp=b_gpp):
                v.wait_ge(tm.h, tm0 + idx + 1)
                xqp[idx % 2].wait(v)
                if idx >= 2:
                    v.wait_ge(gpp[idx % 2].h, b_gpp + 16 * (idx // 2))
                    if stats:
                        v.wait_ge(am.h, b_am + (idx - 1))
                if DBG_RAW:
                    pv.inc(v.tensor_copy(st_, ps[:]))
                else:
                    pv.inc(v.scalar_tensor_tensor(out=st_, in0=ps[:], scalar=0.5, in1=sx,
                                                  op0=ALU.mult, op1=ALU.add))
                if stats:
                    pv.wait(v)
                    dv.inc(v.reduce_sum(sqcol[:, 0:1], st_, axis=AX.X))
                    dv.wait(v)
                    dv.inc(v.tensor_tensor(lnsum[:, c:c + 1], lnsum[:, c:c + 1],
                                           sqcol[:, 0:1], ALU.add))
            blk.vector(ev)

            if stats:
                def sqa(s, st_=st_):
                    pv.wait(s)
                    dv.wait(s)
                    am.inc(s.activation(out=s_sq, in_=st_, func=AF.Square,
                                        accum_out=sqcol[:, 1:2]))
                blk.scalar(sqa)

                def sqv(v, c=c):
                    am.wait(v)
                    dv.inc(v.tensor_tensor(lnsq[:, c:c + 1], lnsq[:, c:c + 1],
                                           sqcol[:, 1:2], ALU.add))
                blk.vector(sqv)

            def spill(g, ft=ft, c=c, idx=idx, st_=st_):
                pv.wait(g)
                gpp[idx % 2].inc(g.dma_start(
                    out=out_dram[c * 128:(c + 1) * 128, ft * 512:(ft + 1) * 512],
                    in_=st_), 16)
            blk.gpsimd(spill)
        return evac

    def make_act_evac(func, spec_fn):
        def evac(ft, c, idx, ps, tm0):
            st_ = s_st[idx % 2]
            dram, colofs = spec_fn(ft)
            b_gpp = BASES["gp0"] if idx % 2 == 0 else BASES["gp1"]

            def ev(s, ft=ft, idx=idx, ps=ps, st_=st_, tm0=tm0, b_gpp=b_gpp):
                s.wait_ge(tm.h, tm0 + idx + 1)
                if idx >= 2:
                    s.wait_ge(gpp[idx % 2].h, b_gpp + 16 * (idx // 2))
                am.inc(s.activation(out=st_.bitcast(BF16)[:, 0:512], in_=ps[:],
                                    func=func))
            blk.scalar(ev)

            def spill(g, c=c, idx=idx, st_=st_, dram=dram, colofs=colofs):
                am.wait(g)
                gpp[idx % 2].inc(g.dma_start(
                    out=dram[c * 128:(c + 1) * 128, colofs:colofs + 512],
                    in_=st_.bitcast(BF16)[:, 0:512]), 16)
            blk.gpsimd(spill)
        return evac

    def make_oproj_evac():
        def evac(ft, c, idx, ps, tm0):
            sx, st_ = s_x0[idx % 2], s_st[idx % 2]
            b_pv, b_am = BASES["pv"], BASES["am"]
            b_gpp = BASES["gp0"] if idx % 2 == 0 else BASES["gp1"]

            def xload(g, ft=ft, c=c, idx=idx, sx=sx):
                if idx >= 2:
                    g.wait_ge(pv.h, b_pv + idx - 1)
                xqp[idx % 2].inc(g.dma_start(
                    out=sx, in_=x1_d[c * 128:(c + 1) * 128, ft * 512:(ft + 1) * 512]), 16)
            blk.gpsimd(xload)

            def ev(v, ft=ft, c=c, idx=idx, ps=ps, sx=sx, st_=st_, tm0=tm0, b_gpp=b_gpp):
                v.wait_ge(tm.h, tm0 + idx + 1)
                xqp[idx % 2].wait(v)
                if idx >= 2:
                    v.wait_ge(gpp[idx % 2].h, b_gpp + 16 * (idx // 2))
                    v.wait_ge(am.h, b_am + (idx - 1))
                pv.inc(v.tensor_tensor(st_, ps[:], sx, ALU.add))
                pv.wait(v)
                dv.inc(v.reduce_sum(sqcol[:, 0:1], st_, axis=AX.X))
                dv.wait(v)
                dv.inc(v.tensor_tensor(lnsum[:, c:c + 1], lnsum[:, c:c + 1],
                                       sqcol[:, 0:1], ALU.add))
            blk.vector(ev)

            def sqa(s, st_=st_):
                pv.wait(s)
                dv.wait(s)
                am.inc(s.activation(out=s_sq, in_=st_, func=AF.Square,
                                    accum_out=sqcol[:, 1:2]))
            blk.scalar(sqa)

            def sqv(v, c=c):
                am.wait(v)
                dv.inc(v.tensor_tensor(lnsq[:, c:c + 1], lnsq[:, c:c + 1],
                                       sqcol[:, 1:2], ALU.add))
            blk.vector(sqv)

            def spill(g, ft=ft, c=c, idx=idx, st_=st_):
                pv.wait(g)
                gpp[idx % 2].inc(g.dma_start(
                    out=x2_d[c * 128:(c + 1) * 128, ft * 512:(ft + 1) * 512],
                    in_=st_), 16)
            blk.gpsimd(spill)
        return evac

    def make_out_evac():
        def evac(ft, c, idx, ps, tm0):
            sx, st_ = s_x0[idx % 2], s_st[idx % 2]
            b_pv = BASES["pv"]
            b_gpp = BASES["gp0"] if idx % 2 == 0 else BASES["gp1"]

            def xload(g, ft=ft, c=c, idx=idx, sx=sx):
                if idx >= 2:
                    g.wait_ge(pv.h, b_pv + idx - 1)
                xqp[idx % 2].inc(g.dma_start(
                    out=sx, in_=x2_d[c * 128:(c + 1) * 128, ft * 512:(ft + 1) * 512]), 16)
            blk.gpsimd(xload)

            def ev(v, ft=ft, c=c, idx=idx, ps=ps, sx=sx, st_=st_, tm0=tm0, b_gpp=b_gpp):
                v.wait_ge(tm.h, tm0 + idx + 1)
                xqp[idx % 2].wait(v)
                if idx >= 2:
                    v.wait_ge(gpp[idx % 2].h, b_gpp + 16 * (idx // 2))
                pv.inc(v.scalar_tensor_tensor(out=st_, in0=ps[:], scalar=0.5, in1=sx,
                                              op0=ALU.mult, op1=ALU.add))
            blk.vector(ev)

            def spill(g, ft=ft, c=c, idx=idx, st_=st_):
                pv.wait(g)
                gpp[idx % 2].inc(g.dma_start(
                    out=y_out[c * 128:(c + 1) * 128, ft * 512:(ft + 1) * 512],
                    in_=st_), 16)
            blk.gpsimd(spill)
        return evac

    def zero_lnstats():
        def z(v):
            v.memset(lnsum[:], 0.0)
            dv.inc(v.memset(lnsq[:], 0.0))
        blk.vector(z)

    def ln_finalize(nch):
        def fin(v):
            dv.inc(v.tensor_scalar(out=lntmp[:, 0:nch], in0=lnsum[:, 0:nch],
                                   scalar1=1.0 / DM, scalar2=None, op0=ALU.mult))
            dv.inc(v.tensor_scalar(out=lnr[:, 0:nch], in0=lnsq[:, 0:nch],
                                   scalar1=1.0 / DM, scalar2=None, op0=ALU.mult))
            dv.wait(v)
            dv.inc(v.tensor_tensor(lnb[:, 0:nch], lntmp[:, 0:nch], lntmp[:, 0:nch],
                                   ALU.mult))
            dv.wait(v)
            dv.inc(v.tensor_tensor(lnsq[:, 0:nch], lnr[:, 0:nch], lnb[:, 0:nch],
                                   ALU.subtract))
            dv.wait(v)
            dv.inc(v.tensor_scalar(out=lnsq[:, 0:nch], in0=lnsq[:, 0:nch], scalar1=EPS,
                                   scalar2=None, op0=ALU.add))
        blk.vector(fin)

        def fin_a(s):
            dv.wait(s)
            am.inc(s.activation(out=lnr[:, 0:nch], in_=lnsq[:, 0:nch], func=AF.Sqrt))
        blk.scalar(fin_a)

        def fin_v(v):
            am.wait(v)
            dv.inc(v.reciprocal(lnsq[:, 0:nch], lnr[:, 0:nch]))
            dv.wait(v)
            dv.inc(v.tensor_copy(lnr[:, 0:nch], lnsq[:, 0:nch]))
            dv.wait(v)
            dv.inc(v.scalar_tensor_tensor(out=lnb[:, 0:nch], in0=lntmp[:, 0:nch],
                                          scalar=-1.0, in1=lnr[:, 0:nch],
                                          op0=ALU.mult, op1=ALU.mult))
        blk.vector(fin_v)
        barrier()

    def kv_accum_scan(ksrc, vsrc, nbl):
        sc_dv = {}
        for b in range(nbl):
            def bload(s, b=b):
                if b >= 1:
                    s.wait_ge(dv.h, sc_dv[b - 1])
                srck = ksrc.rearrange("(bb ct p) n -> bb p ct n", ct=2, p=128)
                dstk = gate_v.rearrange("p (ct n) -> p ct n", n=4096)
                wq.inc(s.dma_start(out=dstk, in_=srck[b]), 16)
                srcv = vsrc.rearrange("(bb ct p) n -> bb p ct n", ct=2, p=128)
                dstv = attn_v.rearrange("p (ct n) -> p ct n", n=4096)
                wq.inc(s.dma_start(out=dstv, in_=srcv[b]), 16)
            blk.sync(bload)

            for pp in range(H // 2):
                hs = (2 * pp, 2 * pp + 1)

                def ktv(v, pp=pp, hs=hs):
                    if pp == 0:
                        wq.wait(v)
                    tm.wait(v)
                    ins = None
                    for j, h in enumerate(hs):
                        for ct in range(2):
                            ins = v.tensor_scalar(
                                out=kt2[:, j * 256 + ct * 128:j * 256 + (ct + 1) * 128],
                                in0=gate_v[:, ct * 4096 + h * 128: ct * 4096 + (h + 1) * 128],
                                scalar1=kdec_s[:, 2 * h + ct:2 * h + ct + 1],
                                scalar2=None, op0=ALU.mult)
                    dv.inc(ins)
                blk.vector(ktv)

                def kvmm(te, pp=pp, hs=hs):
                    dv.wait(te)
                    ins = None
                    for j in range(2):
                        h = hs[j]
                        for ct in range(2):
                            ins = te.matmul(pK[:, j * 128:(j + 1) * 128],
                                            kt2[:, j * 256 + ct * 128:j * 256 + (ct + 1) * 128],
                                            attn_v[:, ct * 4096 + h * 128: ct * 4096 + (h + 1) * 128],
                                            start=(ct == 0 and j == 0),
                                            stop=(ct == 1 and j == 1))
                    tm.inc(ins)
                blk.tensor(kvmm)

                def kvup(v, pp=pp, hs=hs):
                    tm.wait(v)
                    for j in range(2):
                        h = hs[j]
                        dv.inc(v.scalar_tensor_tensor(out=kvst[:, h * 128:(h + 1) * 128],
                                                      in0=kvst[:, h * 128:(h + 1) * 128],
                                                      scalar=bdecs[h], in1=pK[:, j * 128:(j + 1) * 128],
                                                      op0=ALU.mult, op1=ALU.add))
                        dv.wait(v)
                blk.vector(kvup)
            sc_dv[b] = dv.n
        barrier()

    UPTO = cfg.get("upto", 99)
    STEP = [0]

    def take():
        STEP[0] += 1
        return STEP[0] <= UPTO

    if NCP and take():
        ln_phase(x_pref, NCP, TOKP, None)
        zero_lnstats()
        set_bases()
        mm_stage(wdt_d, 8, NCP, TOKP, make_ffn_evac(x_pref, x1p_d, True), pv,
                 bias=True, weng="sync")
        ln_finalize(NCP)
        ln_phase(x1p_d, NCP, TOKP, (lnr, lnb))
        set_bases()

        def kv_spec(ft):
            return (kp_d, ft * 512) if ft < 8 else (vp_d, (ft - 8) * 512)
        mm_stage(wqkvT[:, 4096:12288], 16, NCP, TOKP,
                 make_act_evac(AF.Silu, kv_spec), am, weng="gpsimd")
        kv_accum_scan(kp_d, vp_d, NBP)

        def gmask(v):
            dv.inc(v.tensor_scalar(out=kvst[:], in0=kvst[:], scalar1=gam_s[:],
                                   scalar2=None, op0=ALU.mult))
        blk.vector(gmask)

    def kvb_init(v):
        dv.wait(v)
        dv.inc(v.tensor_copy(kvb[:], kvst[:]))
    blk.vector(kvb_init)
    barrier()

    if take():
        ln_phase(x_own, NCO, TOKO, None)
    if take():
        zero_lnstats()
        set_bases()
        mm_stage(wdt_d, 8, NCO, TOKO, make_ffn_evac(x_own, x1_d, True), pv,
                 bias=not cfg.get("dbg_raw"), weng="sync")
        ln_finalize(NCO)
    if take():
        ln_phase(x1_d, NCO, TOKO, (lnr, lnb))
    if take():
        set_bases()

        def qkv_spec(ft):
            if ft < 8:
                return q_d, ft * 512
            if ft < 16:
                return k_d, (ft - 8) * 512
            return v_d, (ft - 16) * 512
        mm_stage(wqkvT, 24, NCO, TOKO, make_act_evac(AF.Silu, qkv_spec), am,
                 weng="gpsimd")
    if take():
        set_bases()

        def gate_spec(ft):
            return g_d, ft * 512
        mm_stage(wgT, 8, NCO, TOKO, make_act_evac(AF.Sigmoid, gate_spec), am,
                 weng="gpsimd")

    s5_on = take()
    if s5_on:
        def s5_init(v):
            dv.inc(v.memset(rsum[:], 0.0))
        blk.vector(s5_init)

        def s5_dd(s):
            src = ddc.rearrange("(hk p) n -> p hk n", p=128)
            dst = arena[:, 0:16384].rearrange("p (hk n) -> p hk n", n=256)
            wq.inc(s.dma_start(out=dst, in_=src), 16)
        blk.sync(s5_dd)

        smT2 = [s5s[:, 0:512], s5s[:, 512:1024]]
        qkf2 = [s5s[:, 1024:1536], s5s[:, 1536:2048]]
        kt3 = [s5s[:, 2048:2304], s5s[:, 2304:2560], s5s[:, 3584:3840]]
        itmp2 = [s5s[:, 2560:3072].bitcast(F32), s5s[:, 3072:3584].bitcast(F32)]
        pSco = [pS0, pA]
        pIntra = [pC, pD]
        pInter = [pB, pL]
        pTr = [pT[:, 0:512], pK[:].bitcast(BF16)[:, 0:512]]

        QOFS, KOFS, VOFS = 16384, 32768, 49152
        s5_tm_end = {}
        s5_gsp = {}
        for b in range(NBO):
            qo = QOFS + (b % 2) * 8192
            ko = KOFS + (b % 2) * 8192
            vo = VOFS + (b % 2) * 8192

            wqx = wq if b % 2 == 0 else wqb

            def bload(s, b=b, qo=qo, ko=ko, vo=vo, wqx=wqx):
                if b >= 2:
                    s.wait_ge(tm.h, s5_tm_end[b - 2])
                for dram, ofs in ((q_d, qo), (k_d, ko), (v_d, vo)):
                    srcd = dram.rearrange("(bb ct p) n -> bb p ct n", ct=2, p=128)
                    dstd = arena[:, ofs:ofs + 8192].rearrange("p (ct n) -> p ct n", n=4096)
                    wqx.inc(s.dma_start(out=dstd, in_=srcd[b]), 16)
            blk.sync(bload)

            def gload(g, b=b):
                if b >= 1:
                    g.wait_ge(gq.h, s5_gsp[b - 1])
                srcd = g_d.rearrange("(bb ct p) n -> bb p ct n", ct=2, p=128)
                dstg = gate_v.rearrange("p (ct n) -> p ct n", n=4096)
                gq.inc(g.dma_start(out=dstg, in_=srcd[b]), 16)
            blk.gpsimd(gload)

            dv_pre, tm_tr, dv_trev, tm_sco, dv_msk = {}, {}, {}, {}, {}
            tm_amm, am_aev, dv_fin, am_sq, dv_rs = {}, {}, {}, {}, {}

            for i in range(H + 5):
                def vblock1(v, i=i, b=b, ko=ko, wqx=wqx):
                    if i < H:
                        h = i
                        if h == 0:
                            wqx.wait(v)
                        if h >= 3:
                            v.wait_ge(tm.h, tm_amm[h - 3])
                        ins = None
                        for ct in range(2):
                            ins = v.tensor_scalar(
                                out=kt3[h % 3][:, ct * 128:(ct + 1) * 128],
                                in0=arena[:, ko + ct * 4096 + h * 128: ko + ct * 4096 + (h + 1) * 128],
                                scalar1=kdec_s[:, 2 * h + ct:2 * h + ct + 1],
                                scalar2=None, op0=ALU.mult)
                        dv.inc(ins)
                        dv_pre[h] = dv.n
                    if 1 <= i <= H:
                        h = i - 1
                        p_ = h % 2
                        v.wait_ge(tm.h, tm_tr[h])
                        if h >= 2:
                            v.wait_ge(tm.h, tm_amm[h - 2])
                        dv.inc(v.tensor_copy(qkf2[p_][:], pTr[p_][:]))
                        dv_trev[h] = dv.n
                blk.vector(vblock1)

                def tblock(te, i=i, b=b, qo=qo, ko=ko, vo=vo, wqx=wqx):
                    if i < H:
                        h, p_ = i, i % 2
                        te.wait_ge(dv.h, dv_pre[h])
                        if h >= 2:
                            te.wait_ge(dv.h, dv_trev[h - 2])
                        if h == 0:
                            wqx.wait(te)
                        for ct in range(2):
                            te.transpose(pTr[p_][:, ct * 128:(ct + 1) * 128],
                                         arena[:, qo + ct * 4096 + h * 128: qo + ct * 4096 + (h + 1) * 128],
                                         ident[:])
                        ins = None
                        for ct in range(2):
                            ins = te.transpose(pTr[p_][:, 256 + ct * 128:256 + (ct + 1) * 128],
                                               arena[:, ko + ct * 4096 + h * 128: ko + ct * 4096 + (h + 1) * 128],
                                               ident[:])
                        tm.inc(ins)
                        tm_tr[h] = tm.n
                    if 1 <= i <= H:
                        h = i - 1
                        p_ = h % 2
                        te.wait_ge(dv.h, dv_trev[h])
                        if h >= 2:
                            te.wait_ge(dv.h, dv_msk[h - 2])
                        ins = None
                        for ktc in range(2):
                            ins = te.matmul(pSco[p_][:, ktc * 256:(ktc + 1) * 256],
                                            qkf2[p_][:, 256 + ktc * 128:256 + (ktc + 1) * 128],
                                            qkf2[p_][:, 0:256], start=True, stop=True)
                        tm.inc(ins)
                        tm_sco[h] = tm.n
                    if 2 <= i <= H + 1:
                        h = i - 2
                        p_ = h % 2
                        te.wait_ge(dv.h, dv_msk[h])
                        if h >= 2:
                            te.wait_ge(am.h, am_aev[h - 2])
                            te.wait_ge(dv.h, dv_fin[h - 2])
                        te.matmul(pIntra[p_][:, 0:128],
                                  smT2[p_][:, 0:128],
                                  arena[:, vo + h * 128:vo + (h + 1) * 128],
                                  start=True, stop=False)
                        for (ct, ktc) in ((0, 1), (1, 0), (1, 1)):
                            te.matmul(pIntra[p_][:, ct * 128:(ct + 1) * 128],
                                      smT2[p_][:, ktc * 256 + ct * 128: ktc * 256 + (ct + 1) * 128],
                                      arena[:, vo + ktc * 4096 + h * 128: vo + ktc * 4096 + (h + 1) * 128],
                                      start=False, stop=False)
                        te.matmul(pIntra[p_][:, 256:384], kt3[h % 3][:, 0:128],
                                  arena[:, vo + h * 128: vo + (h + 1) * 128],
                                  start=False, stop=False)
                        te.matmul(pIntra[p_][:, 256:384], kt3[h % 3][:, 128:256],
                                  arena[:, vo + 4096 + h * 128: vo + 4096 + (h + 1) * 128],
                                  start=False, stop=True)
                        ins = None
                        for ct in range(2):
                            ins = te.matmul(pInter[p_][:, ct * 128:(ct + 1) * 128],
                                            qkf2[p_][:, ct * 128:(ct + 1) * 128],
                                            kvb[:, h * 128:(h + 1) * 128],
                                            start=(ct == 0), stop=(ct == 1))
                        tm.inc(ins)
                        tm_amm[h] = tm.n
                blk.tensor(tblock)

                def vblock2(v, i=i, b=b):
                    if 1 <= i <= H:
                        h = i - 1
                        p_ = h % 2
                        v.wait_ge(tm.h, tm_sco[h])
                        if h >= 2:
                            v.wait_ge(tm.h, tm_amm[h - 2])
                        ins = None
                        for ktc in range(2):
                            ins = v.tensor_tensor(
                                smT2[p_][:, ktc * 256:(ktc + 1) * 256],
                                pSco[p_][:, ktc * 256:(ktc + 1) * 256],
                                arena[:, h * 512 + ktc * 256: h * 512 + (ktc + 1) * 256],
                                ALU.mult)
                        dv.inc(ins)
                        dv_msk[h] = dv.n
                    if 3 <= i <= H + 2:
                        h = i - 3
                        p_ = h % 2
                        v.wait_ge(am.h, am_aev[h])
                        if h == 0 and b >= 1:
                            v.wait_ge(gq.h, s5_gsp[b - 1])
                        for ct in range(2):
                            v.tensor_tensor(
                                attn_v[:, ct * 4096 + h * 128: ct * 4096 + (h + 1) * 128],
                                pIntra[p_][:, ct * 128:(ct + 1) * 128],
                                itmp2[p_][:, ct * 128:(ct + 1) * 128], ALU.add)
                        dv.inc(v.scalar_tensor_tensor(
                            out=kvst[:, h * 128:(h + 1) * 128],
                            in0=kvst[:, h * 128:(h + 1) * 128],
                            scalar=bdecs[h], in1=pIntra[p_][:, 256:384],
                            op0=ALU.mult, op1=ALU.add))
                        dv.wait(v)
                        dv.inc(v.tensor_copy(kvb[:, h * 128:(h + 1) * 128],
                                             kvst[:, h * 128:(h + 1) * 128]))
                        dv_fin[h] = dv.n
                    if 5 <= i <= H + 4:
                        h = i - 5
                        v.wait_ge(am.h, am_sq[h])
                        if h >= 1:
                            v.wait_ge(dv.h, dv_rs[h - 1])
                        ins = None
                        for ct in range(2):
                            ins = v.tensor_tensor(
                                rsum[:, b * 2 + ct:b * 2 + ct + 1],
                                rsum[:, b * 2 + ct:b * 2 + ct + 1],
                                sqcol[:, (h % 2) * 2 + ct:(h % 2) * 2 + ct + 1], ALU.add)
                        dv.inc(ins)
                        dv_rs[h] = dv.n
                blk.vector(vblock2)

                def ablock(s, i=i, b=b):
                    if 2 <= i <= H + 1:
                        h = i - 2
                        p_ = h % 2
                        s.wait_ge(tm.h, tm_amm[h])
                        if h >= 2:
                            s.wait_ge(dv.h, dv_fin[h - 2])
                        ins = None
                        for ct in range(2):
                            ins = s.activation(
                                out=itmp2[p_][:, ct * 128:(ct + 1) * 128],
                                in_=pInter[p_][:, ct * 128:(ct + 1) * 128],
                                func=AF.Identity,
                                scale=qdec_s[:, 2 * h + ct:2 * h + ct + 1])
                        am.inc(ins)
                        am_aev[h] = am.n
                    if 3 <= i <= H + 2:
                        h = i - 3
                        s.wait_ge(dv.h, dv_fin[h])
                        if h >= 2:
                            s.wait_ge(am.h, am_sq[h - 2])
                            s.wait_ge(dv.h, dv_rs[h - 2])
                        ins = None
                        for ct in range(2):
                            ins = s.activation(
                                out=s_sq.bitcast(BF16)[:, ((h % 2) * 2 + ct) * 256:((h % 2) * 2 + ct) * 256 + 256].bitcast(F32),
                                in_=attn_v[:, ct * 4096 + h * 128: ct * 4096 + (h + 1) * 128],
                                func=AF.Square,
                                accum_out=sqcol[:, (h % 2) * 2 + ct:(h % 2) * 2 + ct + 1])
                        am.inc(ins)
                        am_sq[h] = am.n
                blk.scalar(ablock)
            s5_tm_end[b] = tm.n

            def rstd(v, b=b):
                dv.wait(v)
                v.tensor_scalar(out=t1[:], in0=rsum[:, 2 * b:2 * b + 1], scalar1=1.0 / DM,
                                scalar2=EPS, op0=ALU.mult, op1=ALU.add)
                dv.inc(v.tensor_scalar(out=t2[:], in0=rsum[:, 2 * b + 1:2 * b + 2],
                                       scalar1=1.0 / DM, scalar2=EPS, op0=ALU.mult,
                                       op1=ALU.add))
            blk.vector(rstd)

            def rstd_a(s):
                dv.wait(s)
                s.activation(out=t3[:], in_=t1[:], func=AF.Sqrt)
                am.inc(s.activation(out=t4[:], in_=t2[:], func=AF.Sqrt))
            blk.scalar(rstd_a)

            def gmul(v, b=b):
                am.wait(v)
                gq.wait(v)
                v.reciprocal(t5[:], t3[:])
                dv.inc(v.reciprocal(t6[:], t4[:]))
                dv.wait(v)
                for ct in range(2):
                    rr = t5 if ct == 0 else t6
                    dv.inc(v.tensor_scalar(out=attn_v[:, ct * 4096:(ct + 1) * 4096],
                                           in0=attn_v[:, ct * 4096:(ct + 1) * 4096],
                                           scalar1=rr[:], scalar2=None, op0=ALU.mult))
                    dv.wait(v)
                    dv.inc(v.tensor_tensor(attn_v[:, ct * 4096:(ct + 1) * 4096],
                                           attn_v[:, ct * 4096:(ct + 1) * 4096],
                                           gate_v[:, ct * 4096:(ct + 1) * 4096], ALU.mult))
                    dv.wait(v)
            blk.vector(gmul)

            def gspill(g, b=b):
                dv.wait(g)
                dst = gated_d.rearrange("(bb ct p) n -> bb p ct n", ct=2, p=128)
                srcv = attn_v.rearrange("p (ct n) -> p ct n", n=4096)
                gq.inc(g.dma_start(out=dst[b], in_=srcv), 16)
            blk.gpsimd(gspill)
            s5_gsp[b] = gq.n
        barrier()

    s6p_on = take()
    s6_tm0 = tm.n
    for c in (range(NCO) if s6p_on else []):
        xl = xldA if c % 2 == 0 else xldB

        def gl(g, c=c, xl=xl):
            if c >= 2:
                g.wait_ge(tm.h, s6_tm0 + (c - 1) * 8)
            xqp[c % 2].inc(g.dma_start(out=xl.bitcast(BF16)[:, 0:4096],
                                       in_=gated_d[c * 128:(c + 1) * 128, :]), 16)
        blk.gpsimd(gl)

        for kb in range(8):
            half = kb % 2

            def gt(te, c=c, kb=kb, half=half, xl=xl):
                if kb == 0:
                    xqp[c % 2].wait(te)
                dv.wait(te)
                ins = None
                for k8 in range(4):
                    k = kb * 4 + k8
                    ins = te.transpose(
                        pT[:, half * 512 + k8 * 128: half * 512 + (k8 + 1) * 128],
                        xl.bitcast(BF16)[:, k * 128:(k + 1) * 128], ident[:])
                tm.inc(ins)
            blk.tensor(gt)

            def ge(v, c=c, kb=kb, half=half):
                tm.wait(v)
                ins = None
                for k8 in range(4):
                    k = kb * 4 + k8
                    ins = v.tensor_copy(
                        arena[:, k * TOKO + c * 128: k * TOKO + (c + 1) * 128],
                        pT[:, half * 512 + k8 * 128: half * 512 + (k8 + 1) * 128])
                dv.inc(ins)
            blk.vector(ge)
    barrier()

    if take():
        zero_lnstats()
        set_bases()
        mm_stage(woT, 8, NCO, TOKO, make_oproj_evac(), pv, wscale=True,
                 weng="gpsimd")
        ln_finalize(NCO)
    if take():
        ln_phase(x2_d, NCO, TOKO, (lnr, lnb))
    if take():
        set_bases()
        mm_stage(wdt_d, 8, NCO, TOKO, make_out_evac(), pv, bias=True,
                 weng="sync")
    if arena_out is not None:
        def dmpa(g):
            for sem in (gp0, gp1, gq, xq0, xq1):
                g.wait_ge(sem.h, sem.n)
            for r0 in range(0, 65536, 16384):
                gq.inc(g.dma_start(out=arena_out[:, r0:r0 + 16384],
                                   in_=arena[:, r0:r0 + 16384]), 16)
        blk.gpsimd(dmpa)
    for name, out_ap in dump_outs.items():
        src = scr_map[name]

        def dmp(g, src=src, out_ap=out_ap):
            for sem in (gp0, gp1, gq, xq0, xq1):
                g.wait_ge(sem.h, sem.n)
            rows = src.shape[0]
            step = 512
            for r0 in range(0, rows, step):
                r1 = min(rows, r0 + step)
                gq.inc(g.dma_start(out=out_ap[r0:r1, :], in_=src[r0:r1, :]), 16)
        blk.gpsimd(dmp)
    barrier()

    def fin(s):
        for sem in (wq, wqb, wgq, xq0, xq1, gp0, gp1, gq, cq):
            s.wait_ge(sem.h, sem.n)
    blk.sync(fin)

    blk.__exit__(None, None, None)
    return nc


def host_consts():
    qd, kd, dd, bd = decay_consts()
    ddc = np.ascontiguousarray(
        dd.transpose(0, 2, 1).reshape(H * 256, 256)).astype(ml_dtypes.bfloat16)
    kdec = np.zeros((128, 2 * H), np.float32)
    qdec = np.zeros((128, 2 * H), np.float32)
    for h in range(H):
        for ct in range(2):
            kdec[:, 2 * h + ct] = kd[h, ct * 128:(ct + 1) * 128]
            qdec[:, 2 * h + ct] = qd[h, ct * 128:(ct + 1) * 128]
    ident = np.eye(128, dtype=ml_dtypes.bfloat16)
    return ddc, kdec, qdec, ident


def host_weights(inputs):
    qkv_w = np.asarray(inputs["L0_attn_qkv_w"])
    wqkvT = np.ascontiguousarray(
        qkv_w.reshape(H, 3, D, DM).transpose(1, 0, 2, 3).reshape(3 * DM, DM).T)
    wgT = np.ascontiguousarray(np.asarray(inputs["L0_attn_gate_w"]).T)
    woT = np.ascontiguousarray(np.asarray(inputs["L0_attn_out_w"]).T)
    packT = np.ascontiguousarray(np.asarray(inputs["L0_ffn_packed_weight"]).T)
    scaleT = np.ascontiguousarray(np.asarray(inputs["L0_ffn_scale"]).T)
    fbias = np.asarray(inputs["L0_ffn_bias"])
    fbias_bf = np.ascontiguousarray(fbias[None, :]).astype(ml_dtypes.bfloat16)
    nrmw = np.ascontiguousarray(
        np.asarray(inputs["L0_attn_norm_w"]).reshape(KC, 128).T)
    return wqkvT, wgT, woT, packT, scaleT, fbias_bf, nrmw


def make_in_map(inputs, x_own_np, x_pref_np, gamma_val, cfg):
    wqkvT, wgT, woT, packT, scaleT, fbias_bf, nrmw = host_weights(inputs)
    ddc, kdec, qdec, ident = host_consts()
    return {
        "x_own": np.ascontiguousarray(x_own_np),
        "x_pref": np.ascontiguousarray(x_pref_np),
        "gamma": np.full((128, 1), gamma_val, np.float32),
        "wqkvT": wqkvT, "wgT": wgT, "woT": woT, "packT": packT,
        "scaleT": scaleT, "fbias_bf": fbias_bf, "nrmw": nrmw,
        "ddc": ddc, "kdec": kdec, "qdec": qdec, "identin": ident,
    }


_CFG = {"nco": 16, "ncp": 16, "dump": []}
_NC_CACHE = []


def kernel(**inputs):
    hid = np.asarray(inputs["L0_attn_hidden_states"])
    wqkvT, wgT, woT, packT, scaleT, fbias_bf, nrmw = host_weights(inputs)
    ddc, kdec, qdec, ident = host_consts()
    if not _NC_CACHE:
        _NC_CACHE.append(build(_CFG))
    nc = _NC_CACHE[0]
    in_maps = []
    for c in range(8):
        b, half = c // 2, c % 2
        x_own = np.ascontiguousarray(hid[b, half * 2048:(half + 1) * 2048])
        x_pref = np.ascontiguousarray(hid[b, 0:2048]) if half else x_own
        in_maps.append({
            "x_own": x_own, "x_pref": x_pref,
            "gamma": np.full((128, 1), float(half), np.float32),
            "wqkvT": wqkvT, "wgT": wgT, "woT": woT, "packT": packT,
            "scaleT": scaleT, "fbias_bf": fbias_bf, "nrmw": nrmw,
            "ddc": ddc, "kdec": kdec, "qdec": qdec, "identin": ident,
        })
    res = run_bass_kernel_spmd(nc, in_maps, core_ids=list(range(8)))
    out = np.empty((4, 4096, 4096), np.float32)
    for c in range(8):
        b, half = c // 2, c % 2
        out[b, half * 2048:(half + 1) * 2048] = res.results[c]["y"]
    return out


# version 05
# speedup vs baseline: 1.1274x; 1.0159x over previous
import numpy as np
import ml_dtypes
import concourse.bass as bass
import concourse.mybir as mybir
from concourse.bass_utils import run_bass_kernel_spmd

F32 = mybir.dt.float32
BF16 = mybir.dt.bfloat16
I32 = mybir.dt.int32
AF = mybir.ActivationFunctionType
ALU = mybir.AluOpType
AX = mybir.AxisListType
EPS = 1e-5
H, D, BLKT, DM, KC = 32, 128, 256, 4096, 32


def decay_consts():
    base = 1.0 / 2.0 ** (8.0 / H)
    expo = np.arange(H, dtype=np.float64) + 1.0
    factor = 1.0 - 1 / (32 - 1 + 1e-05) + 1e-05
    slope = base ** expo * factor
    r = np.arange(BLKT, dtype=np.float64) + 1.0
    q_decay = np.exp(-slope[:, None] * r[None, :])
    k_decay = np.exp(-slope[:, None] * (BLKT - r)[None, :])
    diff = r[:, None] - r[None, :]
    diag = slope[:, None, None] * diff[None]
    diag = np.where(diag >= 0, -diag, -np.inf)
    diag_decay = np.exp(diag)
    block_decay = np.exp(-slope * BLKT)
    return q_decay, k_decay, diag_decay, block_decay


class Ct:
    def __init__(self, nc, name):
        self.h = nc.alloc_semaphore(name)
        self.n = 0

    def inc(self, instr, k=1):
        instr.then_inc(self.h, k)
        self.n += k
        return instr

    def wait(self, eng, at=None):
        eng.wait_ge(self.h, self.n if at is None else at)


def build(cfg):
    NCO = cfg.get("nco", 16)
    DBG_RAW = bool(cfg.get("dbg_raw"))
    NCP = cfg.get("ncp", 16)
    dump = cfg.get("dump", [])
    TOKO, TOKP = NCO * 128, NCP * 128
    NBO = TOKO // BLKT
    NBP = TOKP // BLKT if NCP else 0

    _, _, _, bdec_np = decay_consts()
    bdecs = [float(b) for b in bdec_np]

    nc = bass.Bass("TRN2", target_bir_lowering=False, dynamic_dma_scratch_size=16384)

    x_own = nc.dram_tensor("x_own", [TOKO, DM], F32, kind="ExternalInput").ap()
    x_pref = nc.dram_tensor("x_pref", [max(TOKP, 128), DM], F32, kind="ExternalInput").ap()
    gamma = nc.dram_tensor("gamma", [128, 1], F32, kind="ExternalInput").ap()
    wqkvT = nc.dram_tensor("wqkvT", [DM, 12288], F32, kind="ExternalInput").ap()
    wgT = nc.dram_tensor("wgT", [DM, DM], F32, kind="ExternalInput").ap()
    woT = nc.dram_tensor("woT", [DM, DM], F32, kind="ExternalInput").ap()
    packT = nc.dram_tensor("packT", [512, DM], I32, kind="ExternalInput").ap()
    scaleT = nc.dram_tensor("scaleT", [32, DM], F32, kind="ExternalInput").ap()
    nrmw = nc.dram_tensor("nrmw", [128, KC], F32, kind="ExternalInput").ap()
    ddc = nc.dram_tensor("ddc", [H * 256, 256], BF16, kind="ExternalInput").ap()
    kdec = nc.dram_tensor("kdec", [128, 2 * H], F32, kind="ExternalInput").ap()
    qdec = nc.dram_tensor("qdec", [128, 2 * H], F32, kind="ExternalInput").ap()
    ident_in = nc.dram_tensor("identin", [128, 128], BF16, kind="ExternalInput").ap()
    fbias_bf = nc.dram_tensor("fbias_bf", [1, DM], BF16, kind="ExternalInput").ap()
    y_out = nc.dram_tensor("y", [TOKO, DM], F32, kind="ExternalOutput").ap()

    wdt_d = nc.dram_tensor("wdt_d", [DM, DM], BF16).ap()
    wqkv_d = nc.dram_tensor("wqkv_d", [DM, 12288], BF16).ap()
    wg_d = nc.dram_tensor("wg_d", [DM, DM], BF16).ap()
    wo_d = nc.dram_tensor("wo_d", [DM, DM], BF16).ap()
    x1_d = nc.dram_tensor("x1_d", [TOKO, DM], F32).ap()
    x2_d = nc.dram_tensor("x2_d", [TOKO, DM], F32).ap()
    q_d = nc.dram_tensor("q_d", [TOKO, DM], BF16).ap()
    k_d = nc.dram_tensor("k_d", [TOKO, DM], BF16).ap()
    v_d = nc.dram_tensor("v_d", [TOKO, DM], BF16).ap()
    g_d = nc.dram_tensor("g_d", [TOKO, DM], BF16).ap()
    gated_d = nc.dram_tensor("gated_d", [TOKO, DM], BF16).ap()
    scr_map = {"x1_d": x1_d, "x2_d": x2_d, "q_d": q_d, "k_d": k_d, "v_d": v_d,
               "g_d": g_d, "gated_d": gated_d, "wdt_d": wdt_d}
    if NCP:
        x1p_d = nc.dram_tensor("x1p_d", [TOKP, DM], F32).ap()
        kp_d = nc.dram_tensor("kp_d", [TOKP, DM], BF16).ap()
        vp_d = nc.dram_tensor("vp_d", [TOKP, DM], BF16).ap()
        scr_map.update({"x1p_d": x1p_d, "kp_d": kp_d, "vp_d": vp_d})
    dump_outs = {}
    for name in dump:
        if name == "arena":
            continue
        src = scr_map[name]
        dump_outs[name] = nc.dram_tensor(
            "dump_" + name, list(src.shape), src.dtype, kind="ExternalOutput").ap()
    arena_out = (nc.dram_tensor("dump_arena", [128, 65536], BF16,
                                kind="ExternalOutput").ap() if "arena" in dump else None)

    arena = nc.alloc_sbuf_tensor("arena", [128, 65536], BF16)
    kvst = nc.alloc_sbuf_tensor("kvst", [128, 4096], F32)
    kvb = nc.alloc_sbuf_tensor("kvb", [128, 4096], BF16)
    wstream = nc.alloc_sbuf_tensor("wstream", [128, 20480], BF16)
    scratch = nc.alloc_sbuf_tensor("scratch", [128, 2560], F32)
    bn = nc.alloc_sbuf_tensor("bn", [128, 48], F32)
    bnag = nc.alloc_sbuf_tensor("bnag", [128, 2], F32)
    t1 = nc.alloc_sbuf_tensor("t1", [128, 1], F32)
    t2 = nc.alloc_sbuf_tensor("t2", [128, 1], F32)
    t3 = nc.alloc_sbuf_tensor("t3", [128, 1], F32)
    t4 = nc.alloc_sbuf_tensor("t4", [128, 1], F32)
    t5 = nc.alloc_sbuf_tensor("t5", [128, 1], F32)
    t6 = nc.alloc_sbuf_tensor("t6", [128, 1], F32)
    lnsum = nc.alloc_sbuf_tensor("lnsum", [128, 16], F32)
    lnsq = nc.alloc_sbuf_tensor("lnsq", [128, 16], F32)
    lnr = nc.alloc_sbuf_tensor("lnr", [128, 16], F32)
    lnb = nc.alloc_sbuf_tensor("lnb", [128, 16], F32)
    lntmp = nc.alloc_sbuf_tensor("lntmp", [128, 16], F32)
    rsum = nc.alloc_sbuf_tensor("rsum", [128, 16], F32)
    sqcol = nc.alloc_sbuf_tensor("sqcol", [128, 4], F32)
    kdec_s = nc.alloc_sbuf_tensor("kdec_s", [128, 2 * H], F32)
    qdec_s = nc.alloc_sbuf_tensor("qdec_s", [128, 2 * H], F32)
    gam_s = nc.alloc_sbuf_tensor("gam_s", [128, 1], F32)
    nrmw_s = nc.alloc_sbuf_tensor("nrmw_s", [128, KC], F32)
    ident = nc.alloc_sbuf_tensor("ident", [128, 128], BF16)
    ones1 = nc.alloc_sbuf_tensor("ones1", [1, 128], BF16)

    pA = nc.alloc_psum_tensor("pA", [128, 512], F32)
    pB = nc.alloc_psum_tensor("pB", [128, 512], F32)
    pS0 = nc.alloc_psum_tensor("pS0", [128, 512], F32)
    pC = nc.alloc_psum_tensor("pC", [128, 512], F32)
    pD = nc.alloc_psum_tensor("pD", [128, 512], F32)
    pK = nc.alloc_psum_tensor("pK", [128, 512], F32)
    pL = nc.alloc_psum_tensor("pL", [128, 512], F32)
    pT = nc.alloc_psum_tensor("pT", [128, 1024], BF16)
    pTr2 = [pT[:, 0:512], pK[:].bitcast(BF16)[:, 0:512]]

    wgrp = wstream[:, 0:16384].rearrange("p (kk n) -> p kk n", n=512)
    xldA = wstream[:, 0:8192].bitcast(F32)
    xldB = wstream[:, 8192:16384].bitcast(F32)
    xnorm = wstream[:, 16384:20480]
    pk_v = wstream[:, 0:8192].bitcast(I32)
    sh_v = wstream[:, 8192:16384].bitcast(I32)
    wrow_v = wstream[:, 16384:20480]
    biasq = wstream[0:1, 16384:20480]
    gate_v = wstream[:, 0:8192]
    attn_v = wstream[:, 8192:16384]
    s5s = wstream[:, 16384:20480]
    kt2 = s5s[:, 2048:2560]
    smT = s5s[:, 0:512]
    qf = s5s[:, 512:768]
    qs = s5s[:, 768:1024]
    kf = s5s[:, 1024:1280]
    kt_ = s5s[:, 1280:1536]
    qt_ = s5s[:, 1536:1792]
    srep_v = kvst[:, 0:4096]

    s_x0 = [scratch[:, 0:512], scratch[:, 512:1024]]
    s_st = [scratch[:, 1024:1536], scratch[:, 1536:2048]]
    s_sq = scratch[:, 2048:2560]

    bar = Ct(nc, "bar")
    wq = Ct(nc, "wq")
    wqb = Ct(nc, "wqb")
    wgq = Ct(nc, "wgq")
    xq0 = Ct(nc, "xq0")
    xq1 = Ct(nc, "xq1")
    xqp = [xq0, xq1]
    gp0 = Ct(nc, "gp0")
    gp1 = Ct(nc, "gp1")
    gpp = [gp0, gp1]
    gq = Ct(nc, "gq")
    ck = Ct(nc, "ck")
    lq0 = Ct(nc, "lq0")
    lq1 = Ct(nc, "lq1")
    lqp = [lq0, lq1]
    cq = Ct(nc, "cq")
    tm = Ct(nc, "tm")
    dv = Ct(nc, "dv")
    pv = Ct(nc, "pv")
    am = Ct(nc, "am")

    blk = nc.Block()
    blk.__enter__()

    def edge(eng, sem):
        sem.inc(None) if False else None

    def vedge(v):
        pass

    def barrier():
        def dma_fence(s):
            for sem in (wq, wqb, wgq, xq0, xq1, gp0, gp1, gq, cq, lq0, lq1):
                s.wait_ge(sem.h, sem.n)
        blk.sync(dma_fence)
        own = {"tensor": (tm,), "vector": (dv, pv), "scalar": (am,),
               "sync": (), "gpsimd": ()}
        bar.n += 5
        n = bar.n
        for name in ("sync", "tensor", "vector", "scalar", "gpsimd"):
            def f(eng, n=n, name=name):
                for sem in own[name]:
                    eng.wait_ge(sem.h, sem.n)
                eng.nop().then_inc(bar.h, 1)
                eng.wait_ge(bar.h, n)
            getattr(blk, name)(f)

    def ph_consts(s):
        for dst, src in ((kdec_s[:], kdec), (qdec_s[:], qdec), (gam_s[:], gamma),
                         (nrmw_s[:], nrmw), (ident[:], ident_in)):
            cq.inc(s.dma_start(out=dst, in_=src), 16)
    blk.scalar(ph_consts)

    def ph_consts_v(v):
        dv.inc(v.memset(ones1[:], 1.0))
    blk.vector(ph_consts_v)

    deq_dv0 = dv.n
    deq_gp0 = gp0.n
    for cc in range(4):
        def deq_load(s, cc=cc):
            if cc > 0:
                s.wait_ge(dv.h, deq_dv0 + cc * 16)
            wq.inc(s.dma_start(out=pk_v[:], in_=packT[cc * 128:(cc + 1) * 128, :]), 16)
            for g8 in range(8):
                grow = cc * 8 + g8
                wq.inc(s.dma_start(
                    out=srep_v[16 * g8:16 * g8 + 16, :],
                    in_=scaleT[grow:grow + 1, :].broadcast_to((16, DM))), 16)
        blk.sync(deq_load)

        def deq_v(v, cc=cc):
            wq.wait(v)
            for e in range(8):
                dv.inc(v.tensor_scalar(out=sh_v[:], in0=pk_v[:], scalar1=4 * e,
                                       scalar2=28, op0=ALU.logical_shift_left,
                                       op1=ALU.arith_shift_right))
                dv.wait(v)
                if cc * 8 + e >= 1:
                    v.wait_ge(gp0.h, deq_gp0 + (cc * 8 + e) * 16)
                dv.inc(v.scalar_tensor_tensor(out=wrow_v[:], in0=sh_v[:], scalar=1.0,
                                              in1=srep_v[:], op0=ALU.bypass,
                                              op1=ALU.mult))
                dv.wait(v)
        blk.vector(deq_v)

        def deq_spill(g, cc=cc):
            for e in range(8):
                g.wait_ge(dv.h, deq_dv0 + cc * 16 + 2 * e + 2)
                dst = wdt_d.rearrange("(c e) n -> c e n", e=8)[:, e, :]
                gp0.inc(g.dma_start(out=dst[128 * cc:128 * (cc + 1), :], in_=wrow_v[:]), 16)
        blk.gpsimd(deq_spill)

    def kvzero(v):
        gp0.wait(v)
        dv.inc(v.memset(kvst[:], 0.0))
    blk.vector(kvzero)
    barrier()

    CKMARK = {}

    def emit_cast(name, dst, srcw, c0, c1, rstep=128):
        def castf(g, dst=dst, srcw=srcw, c0=c0, c1=c1):
            for r0 in range(0, DM, rstep):
                ck.inc(g.dma_start(out=dst[r0:r0 + rstep, c0:c1],
                                   in_=srcw[r0:r0 + rstep, c0:c1]), 16)
            CKMARK[name] = ck.n
        blk.gpsimd(castf)

    def ln_phase(x_src, nch, tok, use_stats):
        am0, tm0, step = am.n, tm.n, (2 if use_stats is None else 1)
        ln_ev = {}
        for c in range(nch):
            xl = xldA if c % 2 == 0 else xldB

            def load(g, c=c, xl=xl):
                if c >= 2:
                    g.wait_ge(am.h, am0 + step * (c - 1))
                lqp[c % 2].inc(g.dma_start(out=xl[:], in_=x_src[c * 128:(c + 1) * 128, :]), 16)
            blk.sync(load)

            if use_stats is None:
                def stats(v, c=c, xl=xl):
                    lqp[c % 2].wait(v)
                    ins = None
                    for i in range(8):
                        ins = v.bn_stats(bn[:, 6 * i:6 * i + 6], xl[:, 512 * i:512 * (i + 1)])
                    dv.inc(ins)
                    dv.wait(v)
                    dv.inc(v.bn_aggr(bnag[:], bn[:]))
                    dv.wait(v)
                    dv.inc(v.tensor_scalar(out=t1[:], in0=bnag[:, 1:2], scalar1=EPS,
                                           scalar2=None, op0=ALU.add))
                blk.vector(stats)

                def st_a(s):
                    dv.wait(s)
                    am.inc(s.activation(out=t4[:], in_=t1[:], func=AF.Sqrt))
                blk.scalar(st_a)

                def st_v(v):
                    am.wait(v)
                    dv.inc(v.reciprocal(t2[:], t4[:]))
                    dv.wait(v)
                    dv.inc(v.scalar_tensor_tensor(out=t3[:], in0=bnag[:, 0:1],
                                                  scalar=-1.0, in1=t2[:],
                                                  op0=ALU.mult, op1=ALU.mult))
                blk.vector(st_v)
                sc_ap, bi_ap = t2[:], t3[:]
            else:
                lnr_a, lnb_a = use_stats
                sc_ap, bi_ap = lnr_a[:, c:c + 1], lnb_a[:, c:c + 1]

            def norm(s, c=c, xl=xl, sc=sc_ap, bi=bi_ap, us=use_stats):
                if us is None:
                    dv.wait(s)
                else:
                    lqp[c % 2].wait(s)
                if c >= 1:
                    s.wait_ge(tm.h, tm0 + c * 8)
                am.inc(s.activation(out=xnorm[:], in_=xl[:], func=AF.Identity,
                                    bias=bi, scale=sc))
            blk.scalar(norm)

            for kb in range(8):
                half = kb % 2
                bi = c * 8 + kb

                def trans(te, c=c, kb=kb, half=half, bi=bi):
                    if kb == 0:
                        am.wait(te)
                    if bi >= 2:
                        te.wait_ge(dv.h, ln_ev[bi - 2])
                    ins = None
                    for k8 in range(4):
                        k = kb * 4 + k8
                        ins = te.transpose(
                            pTr2[half][:, k8 * 128:(k8 + 1) * 128],
                            xnorm[:, k * 128:(k + 1) * 128], ident[:])
                    tm.inc(ins)
                blk.tensor(trans)

                def evac(v, c=c, kb=kb, half=half, bi=bi):
                    tm.wait(v)
                    ins = None
                    for k8 in range(4):
                        k = kb * 4 + k8
                        ins = v.tensor_copy(
                            arena[:, k * tok + c * 128: k * tok + (c + 1) * 128],
                            pTr2[half][:, k8 * 128:(k8 + 1) * 128])
                    dv.inc(ins)
                    ln_ev[bi] = dv.n
                blk.vector(evac)
        barrier()

    def mm_stage(w_src, n_ft, nch, tok, evac_fn, pfree_sem, bias=False,
                 wscale=False, weng="sync", ckwait=None):
        tm0 = tm.n
        pf_base = pfree_sem.n
        if bias:
            def ldb(s):
                cq.inc(s.dma_start(out=biasq[0:1, 0:DM], in_=fbias_bf), 16)
            blk.scalar(ldb)

        for ft in range(n_ft):
            tmb = tm.n

            def wload_s(eng, ft=ft, tmb=tmb, ckwait=ckwait):
                if ft == 0 and ckwait is not None:
                    eng.wait_ge(ck.h, CKMARK[ckwait])
                if ft > 0:
                    eng.wait_ge(tm.h, tmb)
                src = w_src.rearrange("(kk p) n -> p kk n", p=128)[:, :, ft * 512:(ft + 1) * 512]
                wq.inc(eng.dma_start(out=wgrp[:, 0:16, :], in_=src[:, 0:16, :]), 16)
            blk.sync(wload_s)

            def wload_g(eng, ft=ft, tmb=tmb, ckwait=ckwait):
                if ft == 0 and ckwait is not None:
                    eng.wait_ge(ck.h, CKMARK[ckwait])
                if ft > 0:
                    eng.wait_ge(tm.h, tmb)
                src = w_src.rearrange("(kk p) n -> p kk n", p=128)[:, :, ft * 512:(ft + 1) * 512]
                wgq.inc(eng.dma_start(out=wgrp[:, 16:32, :], in_=src[:, 16:32, :]), 16)
            blk.gpsimd(wload_g)

            if wscale:
                def wsc(v, ft=ft):
                    wq.wait(v)
                    wgq.wait(v)
                    ins = None
                    for k in range(KC):
                        ins = v.tensor_scalar(out=wgrp[:, k, :], in0=wgrp[:, k, :],
                                              scalar1=nrmw_s[:, k:k + 1],
                                              scalar2=None, op0=ALU.mult)
                    dv.inc(ins)
                blk.vector(wsc)

            def mms(te, ft=ft):
                if wscale:
                    dv.wait(te)
                else:
                    wq.wait(te)
                    wgq.wait(te)
                if ft == 0:
                    cq.wait(te)
                for c in range(nch):
                    idx = ft * nch + c
                    ps = pA if idx % 2 == 0 else pB
                    if idx >= 2:
                        te.wait_ge(pfree_sem.h, pf_base + idx - 1)
                    ins = None
                    for k in range(KC):
                        ins = te.matmul(ps[:],
                                        arena[:, k * tok + c * 128:k * tok + (c + 1) * 128],
                                        wgrp[:, k, :], start=(k == 0),
                                        stop=(k == KC - 1 and not bias))
                    if bias:
                        ins = te.matmul(ps[:], ones1[:],
                                        biasq[0:1, ft * 512:(ft + 1) * 512],
                                        start=False, stop=True)
                    tm.inc(ins)
            blk.tensor(mms)

            for c in range(nch):
                idx = ft * nch + c
                ps = pA if idx % 2 == 0 else pB
                evac_fn(ft, c, idx, ps, tm0)
        barrier()

    BASES = {}

    def set_bases():
        BASES.update(pv=pv.n, gp0=gp0.n, gp1=gp1.n, am=am.n)

    def make_ffn_evac(x0_src, out_dram, stats):
        def evac(ft, c, idx, ps, tm0):
            sx, st_ = s_x0[idx % 2], s_st[idx % 2]
            b_pv, b_am = BASES["pv"], BASES["am"]
            b_gpp = BASES["gp0"] if idx % 2 == 0 else BASES["gp1"]

            def xload(g, ft=ft, c=c, idx=idx, sx=sx):
                if idx >= 2:
                    g.wait_ge(pv.h, b_pv + idx - 1)
                xqp[idx % 2].inc(g.dma_start(
                    out=sx, in_=x0_src[c * 128:(c + 1) * 128, ft * 512:(ft + 1) * 512]), 16)
            blk.gpsimd(xload)

            def ev(v, ft=ft, c=c, idx=idx, ps=ps, sx=sx, st_=st_, tm0=tm0, b_gpp=b_gpp):
                v.wait_ge(tm.h, tm0 + idx + 1)
                xqp[idx % 2].wait(v)
                if idx >= 2:
                    v.wait_ge(gpp[idx % 2].h, b_gpp + 16 * (idx // 2))
                    if stats:
                        v.wait_ge(am.h, b_am + (idx - 1))
                if DBG_RAW:
                    pv.inc(v.tensor_copy(st_, ps[:]))
                else:
                    pv.inc(v.scalar_tensor_tensor(out=st_, in0=ps[:], scalar=0.5, in1=sx,
                                                  op0=ALU.mult, op1=ALU.add))
                if stats:
                    pv.wait(v)
                    dv.inc(v.reduce_sum(sqcol[:, 0:1], st_, axis=AX.X))
                    dv.wait(v)
                    dv.inc(v.tensor_tensor(lnsum[:, c:c + 1], lnsum[:, c:c + 1],
                                           sqcol[:, 0:1], ALU.add))
            blk.vector(ev)

            if stats:
                def sqa(s, st_=st_):
                    pv.wait(s)
                    dv.wait(s)
                    am.inc(s.activation(out=s_sq, in_=st_, func=AF.Square,
                                        accum_out=sqcol[:, 1:2]))
                blk.scalar(sqa)

                def sqv(v, c=c):
                    am.wait(v)
                    dv.inc(v.tensor_tensor(lnsq[:, c:c + 1], lnsq[:, c:c + 1],
                                           sqcol[:, 1:2], ALU.add))
                blk.vector(sqv)

            def spill(g, ft=ft, c=c, idx=idx, st_=st_):
                pv.wait(g)
                gpp[idx % 2].inc(g.dma_start(
                    out=out_dram[c * 128:(c + 1) * 128, ft * 512:(ft + 1) * 512],
                    in_=st_), 16)
            blk.gpsimd(spill)
        return evac

    def make_act_evac(func, spec_fn):
        def evac(ft, c, idx, ps, tm0):
            st_ = s_st[idx % 2]
            dram, colofs = spec_fn(ft)
            b_gpp = BASES["gp0"] if idx % 2 == 0 else BASES["gp1"]

            def ev(s, ft=ft, idx=idx, ps=ps, st_=st_, tm0=tm0, b_gpp=b_gpp):
                s.wait_ge(tm.h, tm0 + idx + 1)
                if idx >= 2:
                    s.wait_ge(gpp[idx % 2].h, b_gpp + 16 * (idx // 2))
                am.inc(s.activation(out=st_.bitcast(BF16)[:, 0:512], in_=ps[:],
                                    func=func))
            blk.scalar(ev)

            def spill(g, c=c, idx=idx, st_=st_, dram=dram, colofs=colofs):
                am.wait(g)
                gpp[idx % 2].inc(g.dma_start(
                    out=dram[c * 128:(c + 1) * 128, colofs:colofs + 512],
                    in_=st_.bitcast(BF16)[:, 0:512]), 16)
            blk.gpsimd(spill)
        return evac

    def make_oproj_evac():
        def evac(ft, c, idx, ps, tm0):
            sx, st_ = s_x0[idx % 2], s_st[idx % 2]
            b_pv, b_am = BASES["pv"], BASES["am"]
            b_gpp = BASES["gp0"] if idx % 2 == 0 else BASES["gp1"]

            def xload(g, ft=ft, c=c, idx=idx, sx=sx):
                if idx >= 2:
                    g.wait_ge(pv.h, b_pv + idx - 1)
                xqp[idx % 2].inc(g.dma_start(
                    out=sx, in_=x1_d[c * 128:(c + 1) * 128, ft * 512:(ft + 1) * 512]), 16)
            blk.gpsimd(xload)

            def ev(v, ft=ft, c=c, idx=idx, ps=ps, sx=sx, st_=st_, tm0=tm0, b_gpp=b_gpp):
                v.wait_ge(tm.h, tm0 + idx + 1)
                xqp[idx % 2].wait(v)
                if idx >= 2:
                    v.wait_ge(gpp[idx % 2].h, b_gpp + 16 * (idx // 2))
                    v.wait_ge(am.h, b_am + (idx - 1))
                pv.inc(v.tensor_tensor(st_, ps[:], sx, ALU.add))
                pv.wait(v)
                dv.inc(v.reduce_sum(sqcol[:, 0:1], st_, axis=AX.X))
                dv.wait(v)
                dv.inc(v.tensor_tensor(lnsum[:, c:c + 1], lnsum[:, c:c + 1],
                                       sqcol[:, 0:1], ALU.add))
            blk.vector(ev)

            def sqa(s, st_=st_):
                pv.wait(s)
                dv.wait(s)
                am.inc(s.activation(out=s_sq, in_=st_, func=AF.Square,
                                    accum_out=sqcol[:, 1:2]))
            blk.scalar(sqa)

            def sqv(v, c=c):
                am.wait(v)
                dv.inc(v.tensor_tensor(lnsq[:, c:c + 1], lnsq[:, c:c + 1],
                                       sqcol[:, 1:2], ALU.add))
            blk.vector(sqv)

            def spill(g, ft=ft, c=c, idx=idx, st_=st_):
                pv.wait(g)
                gpp[idx % 2].inc(g.dma_start(
                    out=x2_d[c * 128:(c + 1) * 128, ft * 512:(ft + 1) * 512],
                    in_=st_), 16)
            blk.gpsimd(spill)
        return evac

    def make_out_evac():
        def evac(ft, c, idx, ps, tm0):
            sx, st_ = s_x0[idx % 2], s_st[idx % 2]
            b_pv = BASES["pv"]
            b_gpp = BASES["gp0"] if idx % 2 == 0 else BASES["gp1"]

            def xload(g, ft=ft, c=c, idx=idx, sx=sx):
                if idx >= 2:
                    g.wait_ge(pv.h, b_pv + idx - 1)
                xqp[idx % 2].inc(g.dma_start(
                    out=sx, in_=x2_d[c * 128:(c + 1) * 128, ft * 512:(ft + 1) * 512]), 16)
            blk.gpsimd(xload)

            def ev(v, ft=ft, c=c, idx=idx, ps=ps, sx=sx, st_=st_, tm0=tm0, b_gpp=b_gpp):
                v.wait_ge(tm.h, tm0 + idx + 1)
                xqp[idx % 2].wait(v)
                if idx >= 2:
                    v.wait_ge(gpp[idx % 2].h, b_gpp + 16 * (idx // 2))
                pv.inc(v.scalar_tensor_tensor(out=st_, in0=ps[:], scalar=0.5, in1=sx,
                                              op0=ALU.mult, op1=ALU.add))
            blk.vector(ev)

            def spill(g, ft=ft, c=c, idx=idx, st_=st_):
                pv.wait(g)
                gpp[idx % 2].inc(g.dma_start(
                    out=y_out[c * 128:(c + 1) * 128, ft * 512:(ft + 1) * 512],
                    in_=st_), 16)
            blk.gpsimd(spill)
        return evac

    def zero_lnstats():
        def z(v):
            v.memset(lnsum[:], 0.0)
            dv.inc(v.memset(lnsq[:], 0.0))
        blk.vector(z)

    def ln_finalize(nch):
        def fin(v):
            dv.inc(v.tensor_scalar(out=lntmp[:, 0:nch], in0=lnsum[:, 0:nch],
                                   scalar1=1.0 / DM, scalar2=None, op0=ALU.mult))
            dv.inc(v.tensor_scalar(out=lnr[:, 0:nch], in0=lnsq[:, 0:nch],
                                   scalar1=1.0 / DM, scalar2=None, op0=ALU.mult))
            dv.wait(v)
            dv.inc(v.tensor_tensor(lnb[:, 0:nch], lntmp[:, 0:nch], lntmp[:, 0:nch],
                                   ALU.mult))
            dv.wait(v)
            dv.inc(v.tensor_tensor(lnsq[:, 0:nch], lnr[:, 0:nch], lnb[:, 0:nch],
                                   ALU.subtract))
            dv.wait(v)
            dv.inc(v.tensor_scalar(out=lnsq[:, 0:nch], in0=lnsq[:, 0:nch], scalar1=EPS,
                                   scalar2=None, op0=ALU.add))
        blk.vector(fin)

        def fin_a(s):
            dv.wait(s)
            am.inc(s.activation(out=lnr[:, 0:nch], in_=lnsq[:, 0:nch], func=AF.Sqrt))
        blk.scalar(fin_a)

        def fin_v(v):
            am.wait(v)
            dv.inc(v.reciprocal(lnsq[:, 0:nch], lnr[:, 0:nch]))
            dv.wait(v)
            dv.inc(v.tensor_copy(lnr[:, 0:nch], lnsq[:, 0:nch]))
            dv.wait(v)
            dv.inc(v.scalar_tensor_tensor(out=lnb[:, 0:nch], in0=lntmp[:, 0:nch],
                                          scalar=-1.0, in1=lnr[:, 0:nch],
                                          op0=ALU.mult, op1=ALU.mult))
        blk.vector(fin_v)
        barrier()

    def kv_accum_scan(ksrc, vsrc, nbl):
        sc_dv = {}
        for b in range(nbl):
            def bload(s, b=b):
                if b >= 1:
                    s.wait_ge(dv.h, sc_dv[b - 1])
                srck = ksrc.rearrange("(bb ct p) n -> bb p ct n", ct=2, p=128)
                dstk = gate_v.rearrange("p (ct n) -> p ct n", n=4096)
                wq.inc(s.dma_start(out=dstk, in_=srck[b]), 16)
                srcv = vsrc.rearrange("(bb ct p) n -> bb p ct n", ct=2, p=128)
                dstv = attn_v.rearrange("p (ct n) -> p ct n", n=4096)
                wq.inc(s.dma_start(out=dstv, in_=srcv[b]), 16)
            blk.sync(bload)

            for pp in range(H // 2):
                hs = (2 * pp, 2 * pp + 1)

                def ktv(v, pp=pp, hs=hs):
                    if pp == 0:
                        wq.wait(v)
                    tm.wait(v)
                    ins = None
                    for j, h in enumerate(hs):
                        for ct in range(2):
                            ins = v.tensor_scalar(
                                out=kt2[:, j * 256 + ct * 128:j * 256 + (ct + 1) * 128],
                                in0=gate_v[:, ct * 4096 + h * 128: ct * 4096 + (h + 1) * 128],
                                scalar1=kdec_s[:, 2 * h + ct:2 * h + ct + 1],
                                scalar2=None, op0=ALU.mult)
                    dv.inc(ins)
                blk.vector(ktv)

                def kvmm(te, pp=pp, hs=hs):
                    dv.wait(te)
                    ins = None
                    for j in range(2):
                        h = hs[j]
                        for ct in range(2):
                            ins = te.matmul(pK[:, j * 128:(j + 1) * 128],
                                            kt2[:, j * 256 + ct * 128:j * 256 + (ct + 1) * 128],
                                            attn_v[:, ct * 4096 + h * 128: ct * 4096 + (h + 1) * 128],
                                            start=(ct == 0 and j == 0),
                                            stop=(ct == 1 and j == 1))
                    tm.inc(ins)
                blk.tensor(kvmm)

                def kvup(v, pp=pp, hs=hs):
                    tm.wait(v)
                    for j in range(2):
                        h = hs[j]
                        dv.inc(v.scalar_tensor_tensor(out=kvst[:, h * 128:(h + 1) * 128],
                                                      in0=kvst[:, h * 128:(h + 1) * 128],
                                                      scalar=bdecs[h], in1=pK[:, j * 128:(j + 1) * 128],
                                                      op0=ALU.mult, op1=ALU.add))
                        dv.wait(v)
                blk.vector(kvup)
            sc_dv[b] = dv.n
        barrier()

    UPTO = cfg.get("upto", 99)
    STEP = [0]

    def take():
        STEP[0] += 1
        return STEP[0] <= UPTO

    emit_cast("kv", wqkv_d, wqkvT, 4096, 12288)
    if NCP and take():
        ln_phase(x_pref, NCP, TOKP, None)
        zero_lnstats()
        set_bases()
        mm_stage(wdt_d, 8, NCP, TOKP, make_ffn_evac(x_pref, x1p_d, True), pv,
                 bias=True)
        ln_finalize(NCP)
        ln_phase(x1p_d, NCP, TOKP, (lnr, lnb))
        set_bases()

        def kv_spec(ft):
            return (kp_d, ft * 512) if ft < 8 else (vp_d, (ft - 8) * 512)
        mm_stage(wqkv_d[:, 4096:12288], 16, NCP, TOKP,
                 make_act_evac(AF.Silu, kv_spec), am, ckwait="kv")
        kv_accum_scan(kp_d, vp_d, NBP)

        def gmask(v):
            dv.inc(v.tensor_scalar(out=kvst[:], in0=kvst[:], scalar1=gam_s[:],
                                   scalar2=None, op0=ALU.mult))
        blk.vector(gmask)

    def kvb_init(v):
        dv.wait(v)
        dv.inc(v.tensor_copy(kvb[:], kvst[:]))
    blk.vector(kvb_init)
    barrier()

    emit_cast("q", wqkv_d, wqkvT, 0, 4096)
    emit_cast("g", wg_d, wgT, 0, DM)
    if take():
        ln_phase(x_own, NCO, TOKO, None)
    if take():
        zero_lnstats()
        set_bases()
        mm_stage(wdt_d, 8, NCO, TOKO, make_ffn_evac(x_own, x1_d, True), pv,
                 bias=not cfg.get("dbg_raw"))
        ln_finalize(NCO)
    if take():
        ln_phase(x1_d, NCO, TOKO, (lnr, lnb))
    if take():
        set_bases()

        def qkv_spec(ft):
            if ft < 8:
                return q_d, ft * 512
            if ft < 16:
                return k_d, (ft - 8) * 512
            return v_d, (ft - 16) * 512
        mm_stage(wqkv_d, 24, NCO, TOKO, make_act_evac(AF.Silu, qkv_spec), am,
                 ckwait="g")
    if take():
        set_bases()

        def gate_spec(ft):
            return g_d, ft * 512
        mm_stage(wg_d, 8, NCO, TOKO, make_act_evac(AF.Sigmoid, gate_spec), am,
                 ckwait="g")
    emit_cast("o", wo_d, woT, 0, DM)

    s5_on = take()
    if s5_on:
        def s5_init(v):
            dv.inc(v.memset(rsum[:], 0.0))
        blk.vector(s5_init)

        def s5_dd(s):
            src = ddc.rearrange("(hk p) n -> p hk n", p=128)
            dst = arena[:, 0:16384].rearrange("p (hk n) -> p hk n", n=256)
            wq.inc(s.dma_start(out=dst, in_=src), 16)
        blk.sync(s5_dd)

        smT2 = [s5s[:, 0:512], s5s[:, 512:1024]]
        qkf2 = [s5s[:, 1024:1536], s5s[:, 1536:2048]]
        kt3 = [s5s[:, 2048:2304], s5s[:, 2304:2560], s5s[:, 3584:3840]]
        itmp2 = [s5s[:, 2560:3072].bitcast(F32), s5s[:, 3072:3584].bitcast(F32)]
        pSco = [pS0, pA]
        pIntra = [pC, pD]
        pInter = [pB, pL]
        pTr = [pT[:, 0:512], pK[:].bitcast(BF16)[:, 0:512]]

        QOFS, KOFS, VOFS = 16384, 32768, 49152
        s5_tm_end = {}
        s5_gsp = {}
        for b in range(NBO):
            qo = QOFS + (b % 2) * 8192
            ko = KOFS + (b % 2) * 8192
            vo = VOFS + (b % 2) * 8192

            wqx = wq if b % 2 == 0 else wqb

            def bload(s, b=b, qo=qo, ko=ko, vo=vo, wqx=wqx):
                if b >= 2:
                    s.wait_ge(tm.h, s5_tm_end[b - 2])
                for dram, ofs in ((q_d, qo), (k_d, ko), (v_d, vo)):
                    srcd = dram.rearrange("(bb ct p) n -> bb p ct n", ct=2, p=128)
                    dstd = arena[:, ofs:ofs + 8192].rearrange("p (ct n) -> p ct n", n=4096)
                    wqx.inc(s.dma_start(out=dstd, in_=srcd[b]), 16)
            blk.sync(bload)

            def gload(g, b=b):
                if b >= 1:
                    g.wait_ge(gq.h, s5_gsp[b - 1])
                srcd = g_d.rearrange("(bb ct p) n -> bb p ct n", ct=2, p=128)
                dstg = gate_v.rearrange("p (ct n) -> p ct n", n=4096)
                gq.inc(g.dma_start(out=dstg, in_=srcd[b]), 16)
            blk.gpsimd(gload)

            dv_pre, tm_tr, dv_trev, tm_sco, dv_msk = {}, {}, {}, {}, {}
            tm_amm, am_aev, dv_fin, am_sq, dv_rs = {}, {}, {}, {}, {}

            for i in range(H + 5):
                def vblock1(v, i=i, b=b, ko=ko, wqx=wqx):
                    if i < H:
                        h = i
                        if h == 0:
                            wqx.wait(v)
                        if h >= 3:
                            v.wait_ge(tm.h, tm_amm[h - 3])
                        ins = None
                        for ct in range(2):
                            ins = v.tensor_scalar(
                                out=kt3[h % 3][:, ct * 128:(ct + 1) * 128],
                                in0=arena[:, ko + ct * 4096 + h * 128: ko + ct * 4096 + (h + 1) * 128],
                                scalar1=kdec_s[:, 2 * h + ct:2 * h + ct + 1],
                                scalar2=None, op0=ALU.mult)
                        dv.inc(ins)
                        dv_pre[h] = dv.n
                    if 1 <= i <= H:
                        h = i - 1
                        p_ = h % 2
                        v.wait_ge(tm.h, tm_tr[h])
                        if h >= 2:
                            v.wait_ge(tm.h, tm_amm[h - 2])
                        dv.inc(v.tensor_copy(qkf2[p_][:], pTr[p_][:]))
                        dv_trev[h] = dv.n
                blk.vector(vblock1)

                def tblock(te, i=i, b=b, qo=qo, ko=ko, vo=vo, wqx=wqx):
                    if i < H:
                        h, p_ = i, i % 2
                        te.wait_ge(dv.h, dv_pre[h])
                        if h >= 2:
                            te.wait_ge(dv.h, dv_trev[h - 2])
                        if h == 0:
                            wqx.wait(te)
                        for ct in range(2):
                            te.transpose(pTr[p_][:, ct * 128:(ct + 1) * 128],
                                         arena[:, qo + ct * 4096 + h * 128: qo + ct * 4096 + (h + 1) * 128],
                                         ident[:])
                        ins = None
                        for ct in range(2):
                            ins = te.transpose(pTr[p_][:, 256 + ct * 128:256 + (ct + 1) * 128],
                                               arena[:, ko + ct * 4096 + h * 128: ko + ct * 4096 + (h + 1) * 128],
                                               ident[:])
                        tm.inc(ins)
                        tm_tr[h] = tm.n
                    if 1 <= i <= H:
                        h = i - 1
                        p_ = h % 2
                        te.wait_ge(dv.h, dv_trev[h])
                        if h >= 2:
                            te.wait_ge(dv.h, dv_msk[h - 2])
                        ins = None
                        for ktc in range(2):
                            ins = te.matmul(pSco[p_][:, ktc * 256:(ktc + 1) * 256],
                                            qkf2[p_][:, 256 + ktc * 128:256 + (ktc + 1) * 128],
                                            qkf2[p_][:, 0:256], start=True, stop=True)
                        tm.inc(ins)
                        tm_sco[h] = tm.n
                    if 2 <= i <= H + 1:
                        h = i - 2
                        p_ = h % 2
                        te.wait_ge(dv.h, dv_msk[h])
                        if h >= 2:
                            te.wait_ge(am.h, am_aev[h - 2])
                            te.wait_ge(dv.h, dv_fin[h - 2])
                        te.matmul(pIntra[p_][:, 0:128],
                                  smT2[p_][:, 0:128],
                                  arena[:, vo + h * 128:vo + (h + 1) * 128],
                                  start=True, stop=False)
                        for (ct, ktc) in ((0, 1), (1, 0), (1, 1)):
                            te.matmul(pIntra[p_][:, ct * 128:(ct + 1) * 128],
                                      smT2[p_][:, ktc * 256 + ct * 128: ktc * 256 + (ct + 1) * 128],
                                      arena[:, vo + ktc * 4096 + h * 128: vo + ktc * 4096 + (h + 1) * 128],
                                      start=False, stop=False)
                        te.matmul(pIntra[p_][:, 256:384], kt3[h % 3][:, 0:128],
                                  arena[:, vo + h * 128: vo + (h + 1) * 128],
                                  start=False, stop=False)
                        te.matmul(pIntra[p_][:, 256:384], kt3[h % 3][:, 128:256],
                                  arena[:, vo + 4096 + h * 128: vo + 4096 + (h + 1) * 128],
                                  start=False, stop=True)
                        ins = None
                        for ct in range(2):
                            ins = te.matmul(pInter[p_][:, ct * 128:(ct + 1) * 128],
                                            qkf2[p_][:, ct * 128:(ct + 1) * 128],
                                            kvb[:, h * 128:(h + 1) * 128],
                                            start=(ct == 0), stop=(ct == 1))
                        tm.inc(ins)
                        tm_amm[h] = tm.n
                blk.tensor(tblock)

                def vblock2(v, i=i, b=b):
                    if 1 <= i <= H:
                        h = i - 1
                        p_ = h % 2
                        v.wait_ge(tm.h, tm_sco[h])
                        if h >= 2:
                            v.wait_ge(tm.h, tm_amm[h - 2])
                        ins = None
                        for ktc in range(2):
                            ins = v.tensor_tensor(
                                smT2[p_][:, ktc * 256:(ktc + 1) * 256],
                                pSco[p_][:, ktc * 256:(ktc + 1) * 256],
                                arena[:, h * 512 + ktc * 256: h * 512 + (ktc + 1) * 256],
                                ALU.mult)
                        dv.inc(ins)
                        dv_msk[h] = dv.n
                    if 3 <= i <= H + 2:
                        h = i - 3
                        p_ = h % 2
                        v.wait_ge(am.h, am_aev[h])
                        if h == 0 and b >= 1:
                            v.wait_ge(gq.h, s5_gsp[b - 1])
                        for ct in range(2):
                            v.tensor_tensor(
                                attn_v[:, ct * 4096 + h * 128: ct * 4096 + (h + 1) * 128],
                                pIntra[p_][:, ct * 128:(ct + 1) * 128],
                                itmp2[p_][:, ct * 128:(ct + 1) * 128], ALU.add)
                        dv.inc(v.scalar_tensor_tensor(
                            out=kvst[:, h * 128:(h + 1) * 128],
                            in0=kvst[:, h * 128:(h + 1) * 128],
                            scalar=bdecs[h], in1=pIntra[p_][:, 256:384],
                            op0=ALU.mult, op1=ALU.add))
                        dv.wait(v)
                        dv.inc(v.tensor_copy(kvb[:, h * 128:(h + 1) * 128],
                                             kvst[:, h * 128:(h + 1) * 128]))
                        dv_fin[h] = dv.n
                    if 5 <= i <= H + 4:
                        h = i - 5
                        v.wait_ge(am.h, am_sq[h])
                        if h >= 1:
                            v.wait_ge(dv.h, dv_rs[h - 1])
                        ins = None
                        for ct in range(2):
                            ins = v.tensor_tensor(
                                rsum[:, b * 2 + ct:b * 2 + ct + 1],
                                rsum[:, b * 2 + ct:b * 2 + ct + 1],
                                sqcol[:, (h % 2) * 2 + ct:(h % 2) * 2 + ct + 1], ALU.add)
                        dv.inc(ins)
                        dv_rs[h] = dv.n
                blk.vector(vblock2)

                def ablock(s, i=i, b=b):
                    if 2 <= i <= H + 1:
                        h = i - 2
                        p_ = h % 2
                        s.wait_ge(tm.h, tm_amm[h])
                        if h >= 2:
                            s.wait_ge(dv.h, dv_fin[h - 2])
                        ins = None
                        for ct in range(2):
                            ins = s.activation(
                                out=itmp2[p_][:, ct * 128:(ct + 1) * 128],
                                in_=pInter[p_][:, ct * 128:(ct + 1) * 128],
                                func=AF.Identity,
                                scale=qdec_s[:, 2 * h + ct:2 * h + ct + 1])
                        am.inc(ins)
                        am_aev[h] = am.n
                    if 3 <= i <= H + 2:
                        h = i - 3
                        s.wait_ge(dv.h, dv_fin[h])
                        if h >= 2:
                            s.wait_ge(am.h, am_sq[h - 2])
                            s.wait_ge(dv.h, dv_rs[h - 2])
                        ins = None
                        for ct in range(2):
                            ins = s.activation(
                                out=s_sq.bitcast(BF16)[:, ((h % 2) * 2 + ct) * 256:((h % 2) * 2 + ct) * 256 + 256].bitcast(F32),
                                in_=attn_v[:, ct * 4096 + h * 128: ct * 4096 + (h + 1) * 128],
                                func=AF.Square,
                                accum_out=sqcol[:, (h % 2) * 2 + ct:(h % 2) * 2 + ct + 1])
                        am.inc(ins)
                        am_sq[h] = am.n
                blk.scalar(ablock)
            s5_tm_end[b] = tm.n

            def rstd(v, b=b):
                dv.wait(v)
                v.tensor_scalar(out=t1[:], in0=rsum[:, 2 * b:2 * b + 1], scalar1=1.0 / DM,
                                scalar2=EPS, op0=ALU.mult, op1=ALU.add)
                dv.inc(v.tensor_scalar(out=t2[:], in0=rsum[:, 2 * b + 1:2 * b + 2],
                                       scalar1=1.0 / DM, scalar2=EPS, op0=ALU.mult,
                                       op1=ALU.add))
            blk.vector(rstd)

            def rstd_a(s):
                dv.wait(s)
                s.activation(out=t3[:], in_=t1[:], func=AF.Sqrt)
                am.inc(s.activation(out=t4[:], in_=t2[:], func=AF.Sqrt))
            blk.scalar(rstd_a)

            def gmul(v, b=b):
                am.wait(v)
                gq.wait(v)
                v.reciprocal(t5[:], t3[:])
                dv.inc(v.reciprocal(t6[:], t4[:]))
                dv.wait(v)
                for ct in range(2):
                    rr = t5 if ct == 0 else t6
                    dv.inc(v.tensor_scalar(out=attn_v[:, ct * 4096:(ct + 1) * 4096],
                                           in0=attn_v[:, ct * 4096:(ct + 1) * 4096],
                                           scalar1=rr[:], scalar2=None, op0=ALU.mult))
                    dv.wait(v)
                    dv.inc(v.tensor_tensor(attn_v[:, ct * 4096:(ct + 1) * 4096],
                                           attn_v[:, ct * 4096:(ct + 1) * 4096],
                                           gate_v[:, ct * 4096:(ct + 1) * 4096], ALU.mult))
                    dv.wait(v)
            blk.vector(gmul)

            def gspill(g, b=b):
                dv.wait(g)
                dst = gated_d.rearrange("(bb ct p) n -> bb p ct n", ct=2, p=128)
                srcv = attn_v.rearrange("p (ct n) -> p ct n", n=4096)
                gq.inc(g.dma_start(out=dst[b], in_=srcv), 16)
            blk.gpsimd(gspill)
            s5_gsp[b] = gq.n
        barrier()

    s6p_on = take()
    s6_tm0 = tm.n
    s6_ev = {}
    for c in (range(NCO) if s6p_on else []):
        xl = xldA if c % 2 == 0 else xldB

        def gl(g, c=c, xl=xl):
            if c >= 2:
                g.wait_ge(tm.h, s6_tm0 + (c - 1) * 8)
            lqp[c % 2].inc(g.dma_start(out=xl.bitcast(BF16)[:, 0:4096],
                                       in_=gated_d[c * 128:(c + 1) * 128, :]), 16)
        blk.sync(gl)

        for kb in range(8):
            half = kb % 2
            bi = c * 8 + kb

            def gt(te, c=c, kb=kb, half=half, xl=xl, bi=bi):
                if kb == 0:
                    lqp[c % 2].wait(te)
                if bi >= 2:
                    te.wait_ge(dv.h, s6_ev[bi - 2])
                ins = None
                for k8 in range(4):
                    k = kb * 4 + k8
                    ins = te.transpose(
                        pTr2[half][:, k8 * 128:(k8 + 1) * 128],
                        xl.bitcast(BF16)[:, k * 128:(k + 1) * 128], ident[:])
                tm.inc(ins)
            blk.tensor(gt)

            def ge(v, c=c, kb=kb, half=half, bi=bi):
                tm.wait(v)
                ins = None
                for k8 in range(4):
                    k = kb * 4 + k8
                    ins = v.tensor_copy(
                        arena[:, k * TOKO + c * 128: k * TOKO + (c + 1) * 128],
                        pTr2[half][:, k8 * 128:(k8 + 1) * 128])
                dv.inc(ins)
                s6_ev[bi] = dv.n
            blk.vector(ge)
    barrier()

    if take():
        zero_lnstats()
        set_bases()
        mm_stage(wo_d, 8, NCO, TOKO, make_oproj_evac(), pv, wscale=True,
                 ckwait="o")
        ln_finalize(NCO)
    if take():
        ln_phase(x2_d, NCO, TOKO, (lnr, lnb))
    if take():
        set_bases()
        mm_stage(wdt_d, 8, NCO, TOKO, make_out_evac(), pv, bias=True)
    if arena_out is not None:
        def dmpa(g):
            for sem in (gp0, gp1, gq, xq0, xq1):
                g.wait_ge(sem.h, sem.n)
            for r0 in range(0, 65536, 16384):
                gq.inc(g.dma_start(out=arena_out[:, r0:r0 + 16384],
                                   in_=arena[:, r0:r0 + 16384]), 16)
        blk.gpsimd(dmpa)
    for name, out_ap in dump_outs.items():
        src = scr_map[name]

        def dmp(g, src=src, out_ap=out_ap):
            for sem in (gp0, gp1, gq, xq0, xq1):
                g.wait_ge(sem.h, sem.n)
            rows = src.shape[0]
            step = 512
            for r0 in range(0, rows, step):
                r1 = min(rows, r0 + step)
                gq.inc(g.dma_start(out=out_ap[r0:r1, :], in_=src[r0:r1, :]), 16)
        blk.gpsimd(dmp)
    barrier()

    def fin(s):
        for sem in (wq, wqb, wgq, xq0, xq1, gp0, gp1, gq, cq, ck, lq0, lq1):
            s.wait_ge(sem.h, sem.n)
    blk.sync(fin)

    blk.__exit__(None, None, None)
    return nc


def host_consts():
    qd, kd, dd, bd = decay_consts()
    ddc = np.ascontiguousarray(
        dd.transpose(0, 2, 1).reshape(H * 256, 256)).astype(ml_dtypes.bfloat16)
    kdec = np.zeros((128, 2 * H), np.float32)
    qdec = np.zeros((128, 2 * H), np.float32)
    for h in range(H):
        for ct in range(2):
            kdec[:, 2 * h + ct] = kd[h, ct * 128:(ct + 1) * 128]
            qdec[:, 2 * h + ct] = qd[h, ct * 128:(ct + 1) * 128]
    ident = np.eye(128, dtype=ml_dtypes.bfloat16)
    return ddc, kdec, qdec, ident


def host_weights(inputs):
    qkv_w = np.asarray(inputs["L0_attn_qkv_w"])
    wqkvT = np.ascontiguousarray(
        qkv_w.reshape(H, 3, D, DM).transpose(1, 0, 2, 3).reshape(3 * DM, DM).T)
    wgT = np.ascontiguousarray(np.asarray(inputs["L0_attn_gate_w"]).T)
    woT = np.ascontiguousarray(np.asarray(inputs["L0_attn_out_w"]).T)
    packT = np.ascontiguousarray(np.asarray(inputs["L0_ffn_packed_weight"]).T)
    scaleT = np.ascontiguousarray(np.asarray(inputs["L0_ffn_scale"]).T)
    fbias = np.asarray(inputs["L0_ffn_bias"])
    fbias_bf = np.ascontiguousarray(fbias[None, :]).astype(ml_dtypes.bfloat16)
    nrmw = np.ascontiguousarray(
        np.asarray(inputs["L0_attn_norm_w"]).reshape(KC, 128).T)
    return wqkvT, wgT, woT, packT, scaleT, fbias_bf, nrmw


def make_in_map(inputs, x_own_np, x_pref_np, gamma_val, cfg):
    wqkvT, wgT, woT, packT, scaleT, fbias_bf, nrmw = host_weights(inputs)
    ddc, kdec, qdec, ident = host_consts()
    return {
        "x_own": np.ascontiguousarray(x_own_np),
        "x_pref": np.ascontiguousarray(x_pref_np),
        "gamma": np.full((128, 1), gamma_val, np.float32),
        "wqkvT": wqkvT, "wgT": wgT, "woT": woT, "packT": packT,
        "scaleT": scaleT, "fbias_bf": fbias_bf, "nrmw": nrmw,
        "ddc": ddc, "kdec": kdec, "qdec": qdec, "identin": ident,
    }


_CFG = {"nco": 16, "ncp": 16, "dump": []}
_NC_CACHE = []


def kernel(**inputs):
    hid = np.asarray(inputs["L0_attn_hidden_states"])
    wqkvT, wgT, woT, packT, scaleT, fbias_bf, nrmw = host_weights(inputs)
    ddc, kdec, qdec, ident = host_consts()
    if not _NC_CACHE:
        _NC_CACHE.append(build(_CFG))
    nc = _NC_CACHE[0]
    in_maps = []
    for c in range(8):
        b, half = c // 2, c % 2
        x_own = np.ascontiguousarray(hid[b, half * 2048:(half + 1) * 2048])
        x_pref = np.ascontiguousarray(hid[b, 0:2048]) if half else x_own
        in_maps.append({
            "x_own": x_own, "x_pref": x_pref,
            "gamma": np.full((128, 1), float(half), np.float32),
            "wqkvT": wqkvT, "wgT": wgT, "woT": woT, "packT": packT,
            "scaleT": scaleT, "fbias_bf": fbias_bf, "nrmw": nrmw,
            "ddc": ddc, "kdec": kdec, "qdec": qdec, "identin": ident,
        })
    res = run_bass_kernel_spmd(nc, in_maps, core_ids=list(range(8)))
    out = np.empty((4, 4096, 4096), np.float32)
    for c in range(8):
        b, half = c // 2, c % 2
        out[b, half * 2048:(half + 1) * 2048] = res.results[c]["y"]
    return out
